# version 05
# baseline (speedup 1.0000x reference)
import numpy as np
import ml_dtypes
from contextlib import ExitStack

import concourse.bass as bass
import concourse.mybir as mybir
from concourse.bass_utils import run_bass_kernel_spmd

F32 = mybir.dt.float32
BF16 = mybir.dt.bfloat16
I32 = mybir.dt.int32
AF = mybir.ActivationFunctionType
ALU = mybir.AluOpType
AX = mybir.AxisListType

D = 2048
NCH = 16
TC = 1024
T = 2048
NST = 16
DTR = 128
DFF = 1408
NFC = 11
NEXP = 18
NH = 16
NKV = 4
HD = 64
MLPD = 4096
LN_EPS = 1e-5
EXP_SHIFT = -10.0

BF = ml_dtypes.bfloat16
ENGS = ("sync", "gpsimd", "tensor", "vector", "scalar")

_NM_CTR = [0]


def _nm(name):
    _NM_CTR[0] += 1
    return f"{name}_{_NM_CTR[0]}"



class Prog:

    def __init__(self, nc, ctx):
        self.nc = nc
        self.ctx = ctx
        self.steps = {e: [] for e in ENGS}
        self.sem = {e: self._mk(nc, ctx, "s_" + e) for e in ENGS}
        self._fence_pending = {}

    def fence(self):
        snap = {e: self.sem[e].n for e in ENGS}
        self._fence_pending = {e: snap for e in ENGS}

    @staticmethod
    def _mk(nc, ctx, name):
        class _S:
            pass
        s = _S()
        s.h = ctx.enter_context(nc.semaphore(name))
        s.n = 0
        return s

    def step(self, eng, waits, f, inc=1):
        snap = []
        own = self.sem[eng]
        if eng != "tensor" and inc < 16 and own.n > 0:
            snap.append((own, own.n))
        if eng in self._fence_pending:
            fsnap = self._fence_pending.pop(eng)
            for o in ENGS:
                if o != eng and fsnap[o] > 0:
                    snap.append((self.sem[o], fsnap[o]))
        for w in waits:
            if isinstance(w, tuple):
                s, n = self.sem[w[0]], w[1]
            else:
                s, n = self.sem[w], self.sem[w].n
            if n > 0 and not (s is own and snap and snap[0][0] is own):
                snap.append((s, n))
        self.steps[eng].append((snap, f, own, inc))
        own.n += inc
        return own.n

    def n(self, eng):
        return self.sem[eng].n

    def emit(self, block):
        for eng in ENGS:
            if not self.steps[eng]:
                continue
            steps = self.steps[eng]

            def body(e, steps=steps):
                for snap, f, own, inc in steps:
                    for s, cnt in snap:
                        e.wait_ge(s.h, cnt)
                    last = f(e)
                    if last is not None and inc > 0:
                        last.then_inc(own.h, inc)
            getattr(block, eng)(body)


def _ln_featmajor(p, src, out_bf, t_len, ones_f, ones_rf, rows, bcast, ps, sq_t,
                  eps_col, extra_wait=()):
    ntt = (t_len + 511) // 512
    for tt in range(ntt):
        w = min(512, t_len - tt * 512)
        sl = slice(tt * 512, tt * 512 + w)
        for c in range(NCH):
            p.step("scalar", ["tensor"] + list(extra_wait),
                   lambda s, c=c, sl=sl: s.activation(sq_t[:, sl], src(c)[:, sl], AF.Square))
            p.step("tensor", ["scalar"] + list(extra_wait),
                   lambda tn, c=c, sl=sl, tt=tt, w=w: (
                       tn.matmul(ps[0][0:1, 0:w], ones_f[:], src(c)[:, sl],
                                 start=(c == 0), stop=(c == NCH - 1),
                                 skip_group_check=True),
                       tn.matmul(ps[1][0:1, 0:w], ones_f[:], sq_t[:, sl],
                                 start=(c == 0), stop=(c == NCH - 1),
                                 skip_group_check=True))[-1])
        def fin_v(v, sl=sl, w=w):
            v.tensor_scalar_mul(rows["m"][:, sl], ps[0][0:1, 0:w], 1.0 / D)
            return v.tensor_scalar_mul(rows["var"][:, sl], ps[1][0:1, 0:w], 1.0 / D)
        p.step("vector", ["tensor"], fin_v)
        p.step("vector", [],
               lambda v, sl=sl: v.tensor_mul(rows["mm"][:, sl], rows["m"][:, sl], rows["m"][:, sl]))
        p.step("vector", [],
               lambda v, sl=sl: v.tensor_sub(rows["var"][:, sl], rows["var"][:, sl], rows["mm"][:, sl]))
        p.step("scalar", ["vector"],
               lambda s, sl=sl: s.activation(rows["rstd"][:, sl], rows["var"][:, sl], AF.Ln, bias=eps_col[0:1, :]))
        p.step("scalar", ["scalar"],
               lambda s, sl=sl: s.activation(rows["rstd"][:, sl], rows["rstd"][:, sl], AF.Exp, scale=-0.5))
        p.step("vector", ["scalar"],
               lambda v, sl=sl: v.scalar_tensor_tensor(
                   rows["nmr"][:, sl], rows["m"][:, sl], -1.0, rows["rstd"][:, sl],
                   ALU.mult, ALU.mult))
        p.step("tensor", ["vector"],
               lambda tn, sl=sl, w=w: (
                   tn.matmul(ps[0][:, 0:w], ones_rf[:], rows["rstd"][:, sl],
                             start=True, stop=True),
                   tn.matmul(ps[1][:, 0:w], ones_rf[:], rows["nmr"][:, sl],
                             start=True, stop=True))[-1])
        p.step("scalar", ["tensor"],
               lambda s, sl=sl, w=w: (
                   s.copy(bcast[0][:, sl], ps[0][:, 0:w]),
                   s.copy(bcast[1][:, sl], ps[1][:, 0:w]))[-1])
    for c in range(NCH):
        p.step("vector", ["scalar"],
               lambda v, c=c: v.tensor_mul(sq_t[:], src(c), bcast[0][:, 0:t_len]))
        p.step("vector", [],
               lambda v, c=c: v.tensor_add(out_bf(c), sq_t[:], bcast[1][:, 0:t_len]))


def _softplus(p, z_src, out, tmp1, tmp2):
    p.step("scalar", ["vector"], lambda s: s.activation(tmp1[:], z_src, AF.Abs))
    p.step("scalar", ["scalar"], lambda s: s.activation(tmp1[:], tmp1[:], AF.Exp, scale=-1.0))
    p.step("scalar", ["scalar"], lambda s: s.activation(tmp1[:], tmp1[:], AF.Ln, bias=1.0))
    p.step("vector", ["scalar"], lambda v: v.tensor_scalar_max(tmp2[:], z_src, 0.0))
    p.step("vector", ["vector"], lambda v: v.tensor_add(out[:], tmp1[:], tmp2[:]))



def build_A(stages=99, p_ext=None, IO=None):
    if p_ext is not None:
        nc = p_ext.nc
    else:
        nc = bass.Bass("TRN2", target_bir_lowering=False)

    def T_in(name, shape, dtype):
        return IO[name] if IO is not None else nc.dram_tensor(name, shape, dtype, kind="ExternalInput")

    def T_out(name, shape, dtype):
        return IO[name] if IO is not None else nc.dram_tensor(name, shape, dtype, kind="ExternalOutput")
    xT = T_in("xT", [D, TC], F32)
    xhT = T_in("xhT", [D, 3], F32)
    conv_w4 = T_in("conv_w4", [D, 4], F32)
    conv_b = T_in("conv_b", [128, NCH], F32)
    xprojT = T_in("xprojT", [D, 160], F32)
    dtprojT = T_in("dtprojT", [DTR, D], F32)
    dtproj_b = T_in("dtproj_b", [128, NCH], F32)
    A_log = T_in("A_log", [D, NST], F32)

    o_xn = T_out("o_xn", [D, TC], BF16)
    o_xc = T_out("o_xc", [D, TC], BF16)
    o_dl = T_out("o_dl", [D, TC], F32)
    o_bc = T_out("o_bc", [32, TC], F32)
    o_hf = T_out("o_hf", [128, NCH * NST], F32)

    with ExitStack() as ctx:
        e = ctx.enter_context
        buf0 = e(nc.sbuf_tensor(_nm("buf0"), [128, NCH, TC], F32))
        xn = e(nc.sbuf_tensor(_nm("xn"), [128, NCH, TC], BF16))
        xcb = e(nc.sbuf_tensor(_nm("xcb"), [128, NCH, TC], BF16))
        tA = e(nc.sbuf_tensor(_nm("tA"), [128, TC], F32))
        tB = e(nc.sbuf_tensor(_nm("tB"), [128, TC], F32))
        tC = e(nc.sbuf_tensor(_nm("tC"), [128, TC], F32))
        tD = e(nc.sbuf_tensor(_nm("tD"), [128, TC], F32))
        tE = e(nc.sbuf_tensor(_nm("tE"), [128, TC], F32))
        a_t = [e(nc.sbuf_tensor(_nm(f"a_t{i}"), [128, TC], F32)) for i in range(2)]
        xh = e(nc.sbuf_tensor(_nm("xh"), [128, NCH, 3], F32))
        xh_bf = e(nc.sbuf_tensor(_nm("xh_bf"), [128, NCH, 3], BF16))
        cw = e(nc.sbuf_tensor(_nm("cw"), [128, NCH, 4], F32))
        cb = e(nc.sbuf_tensor(_nm("cb"), [128, NCH], F32))
        dpb = e(nc.sbuf_tensor(_nm("dpb"), [128, NCH], F32))
        Asb = e(nc.sbuf_tensor(_nm("Asb"), [128, NCH, NST], F32))
        xprojT_bf = e(nc.sbuf_tensor(_nm("xprojT_bf"), [128, NCH, 160], BF16))
        dtprojT_bf = e(nc.sbuf_tensor(_nm("dtprojT_bf"), [128, D], BF16))
        ones_bf = e(nc.sbuf_tensor(_nm("ones_bf"), [1, 128], BF16))
        ones_f = e(nc.sbuf_tensor(_nm("ones_f"), [128, 1], F32))
        ones_rf = e(nc.sbuf_tensor(_nm("ones_rf"), [1, 128], F32))
        eps_col = e(nc.sbuf_tensor(_nm("eps_col"), [128, 1], F32))
        rm = e(nc.sbuf_tensor(_nm("rm"), [1, TC], F32))
        rv = e(nc.sbuf_tensor(_nm("rv"), [1, TC], F32))
        rn = e(nc.sbuf_tensor(_nm("rn"), [1, TC], F32))
        rows = {"m": rm, "var": rv, "mm": rn, "rstd": rv, "nmr": rn}
        rstd_b = e(nc.sbuf_tensor(_nm("rstd_b"), [128, TC], BF16))
        nmr_b = e(nc.sbuf_tensor(_nm("nmr_b"), [128, TC], BF16))
        xe = e(nc.sbuf_tensor(_nm("xe"), [128, TC + 3], BF16))
        dt_bf = e(nc.sbuf_tensor(_nm("dt_bf"), [128, TC], BF16))
        bc_f = e(nc.sbuf_tensor(_nm("bc_f"), [32, TC], F32))
        bc_bf = e(nc.sbuf_tensor(_nm("bc_bf"), [32, TC], BF16))
        selB = e(nc.sbuf_tensor(_nm("selB"), [32, NST, 128], BF16))
        selC = e(nc.sbuf_tensor(_nm("selC"), [32, NST, 128], BF16))
        hfin = e(nc.sbuf_tensor(_nm("hfin"), [128, NCH * NST], F32))
        ps = [e(nc.psum_tensor(_nm(f"ps{i}"), [128, 512], F32)) for i in range(8)]

        p = p_ext if p_ext is not None else Prog(nc, ctx)
        if p_ext is not None:
            p.fence()

        for f in [
            lambda s: s.dma_start(buf0[:], xT.ap().rearrange("(c p) t -> p c t", p=128)),
            lambda s: s.dma_start(xh[:], xhT.ap().rearrange("(c p) t -> p c t", p=128)),
            lambda s: s.dma_start(cw[:], conv_w4.ap().rearrange("(c p) k -> p c k", p=128)),
            lambda s: s.dma_start(cb[:], conv_b.ap()),
            lambda s: s.dma_start(dpb[:], dtproj_b.ap()),
            lambda s: s.dma_start(Asb[:], A_log.ap().rearrange("(c p) n -> p c n", p=128)),
        ]:
            p.step("sync", [], f, inc=16)
        p.step("gpsimd", [], lambda g: g.dma_start(
            xprojT_bf[:], xprojT.ap().rearrange("(c p) f -> p c f", p=128)), inc=16)
        p.step("gpsimd", [], lambda g: g.dma_start(dtprojT_bf[:], dtprojT.ap()), inc=16)

        def consts(g):
            g.memset(ones_bf[:], 1.0)
            g.memset(ones_rf[:], 1.0)
            g.memset(ones_f[:], 1.0)
            g.memset(eps_col[:], LN_EPS)
            g.memset(selB[:], 0.0)
            return g.memset(selC[:], 0.0)
        p.step("gpsimd", [], consts)
        p.step("gpsimd", [], lambda g: g.affine_select(
            out=selB[:], in_=selB[:], compare_op=ALU.not_equal, fill=1.0, base=0,
            pattern=[[-1, NST], [0, 128]], channel_multiplier=1))
        p.step("gpsimd", [], lambda g: g.affine_select(
            out=selC[:], in_=selC[:], compare_op=ALU.not_equal, fill=1.0, base=-16,
            pattern=[[-1, NST], [0, 128]], channel_multiplier=1))

        p.step("scalar", ["sync"], lambda s: s.activation(Asb[:], Asb[:], AF.Exp))
        p.step("scalar", ["scalar"], lambda s: s.mul(Asb[:], Asb[:], -1.0))

        p.step("vector", ["sync", "gpsimd"], lambda v: None, inc=0)
        _ln_featmajor(p, lambda c: buf0[:, c, :], lambda c: xn[:, c, :], TC,
                      ones_f, ones_rf, rows, (rstd_b, nmr_b), (ps[0], ps[1]), tA,
                      eps_col, extra_wait=["sync", "gpsimd"])

        if stages >= 2:
            _build_A_halo(p, locals())
        if stages >= 3:
            _build_A_conv(p, locals())
        if stages >= 4:
            _build_A_scan(p, locals(), stages)
        _build_A_stores(p, locals(), stages)
        if p_ext is None:
            block = e(nc.Block())
            p.emit(block)
    return nc


def _build_A_halo(p, L):
    nc = L["nc"]; tA = L["tA"]; xh = L["xh"]; xh_bf = L["xh_bf"]
    ones_f = L["ones_f"]; ones_rf = L["ones_rf"]; ps = L["ps"]
    rm = L["rm"]; rv = L["rv"]; rn = L["rn"]
    rstd_b = L["rstd_b"]; nmr_b = L["nmr_b"]; eps_col = L["eps_col"]
    if True:
        for c in range(NCH):
            p.step("scalar", ["tensor", "vector"],
                   lambda s, c=c: s.activation(tA[:, c * 3:(c + 1) * 3], xh[:, c, :], AF.Square))
        for c in range(NCH):
            p.step("tensor", ["scalar"],
                   lambda tn, c=c: (
                       tn.matmul(ps[0][0:1, 0:3], ones_f[:], xh[:, c, :],
                                 start=(c == 0), stop=(c == NCH - 1),
                                 skip_group_check=True),
                       tn.matmul(ps[1][0:1, 0:3], ones_f[:], tA[:, c * 3:(c + 1) * 3],
                                 start=(c == 0), stop=(c == NCH - 1),
                                 skip_group_check=True))[-1])
        def fin_h(v):
            v.tensor_scalar_mul(rm[:, 0:3], ps[0][0:1, 0:3], 1.0 / D)
            return v.tensor_scalar_mul(rv[:, 0:3], ps[1][0:1, 0:3], 1.0 / D)
        p.step("vector", ["tensor"], fin_h)
        p.step("vector", [], lambda v: v.tensor_mul(rn[:, 0:3], rm[:, 0:3], rm[:, 0:3]))
        p.step("vector", [], lambda v: v.tensor_sub(rv[:, 0:3], rv[:, 0:3], rn[:, 0:3]))
        p.step("scalar", ["vector"], lambda s: s.activation(rv[:, 0:3], rv[:, 0:3], AF.Ln, bias=eps_col[0:1, :]))
        p.step("scalar", ["scalar"], lambda s: s.activation(rv[:, 0:3], rv[:, 0:3], AF.Exp, scale=-0.5))
        p.step("vector", ["scalar"],
               lambda v: v.scalar_tensor_tensor(rn[:, 0:3], rm[:, 0:3], -1.0, rv[:, 0:3],
                                                ALU.mult, ALU.mult))
        p.step("tensor", ["vector"],
               lambda tn: (
                   tn.matmul(ps[0][:, 0:3], ones_rf[:], rv[:, 0:3], start=True, stop=True),
                   tn.matmul(ps[1][:, 0:3], ones_rf[:], rn[:, 0:3], start=True, stop=True))[-1])
        p.step("scalar", ["tensor"],
               lambda s: (s.copy(rstd_b[:, 0:3], ps[0][:, 0:3]),
                          s.copy(nmr_b[:, 0:3], ps[1][:, 0:3]))[-1])
        for c in range(NCH):
            p.step("vector", ["scalar"],
                   lambda v, c=c: v.tensor_mul(tA[:, 0:3], xh[:, c, :], rstd_b[:, 0:3]))
            p.step("vector", [],
                   lambda v, c=c: v.tensor_add(xh_bf[:, c, :], tA[:, 0:3], nmr_b[:, 0:3]))



def _build_A_conv(p, L):
    nc = L["nc"]; tB = L["tB"]; tC = L["tC"]; tD = L["tD"]
    xh_bf = L["xh_bf"]; xn = L["xn"]; xe = L["xe"]; cw = L["cw"]; cb = L["cb"]
    xcb = L["xcb"]; xprojT_bf = L["xprojT_bf"]; ps = L["ps"]
    dt_bf = L["dt_bf"]; bc_f = L["bc_f"]; bc_bf = L["bc_bf"]
    if True:
        acc = [tB, tC]
        for c in range(NCH):
            par = c % 2
            def conv_cp(v, c=c):
                v.tensor_copy(xe[:, 0:3], xh_bf[:, c, :])
                return v.tensor_copy(xe[:, 3:TC + 3], xn[:, c, :])
            p.step("vector", ["scalar"], conv_cp)
            p.step("vector", [],
                   lambda v, c=c, par=par: v.tensor_scalar(
                       acc[par][:], xe[:, 3:TC + 3], cw[:, c, 3:4], None, ALU.mult))
            for k in range(3):
                p.step("vector", [],
                       lambda v, c=c, par=par, k=k: v.scalar_tensor_tensor(
                           acc[par][:], xe[:, k:k + TC], cw[:, c, k:k + 1], acc[par][:],
                           ALU.mult, ALU.add))
            p.step("scalar", ["vector"],
                   lambda s, c=c, par=par: s.activation(xcb[:, c, :], acc[par][:],
                                                        AF.Silu, bias=cb[:, c:c + 1]))
            def xproj_mm(tn, c=c):
                for tt in range(2):
                    sl = slice(tt * 512, (tt + 1) * 512)
                    tn.matmul(ps[2 + tt][:], xprojT_bf[:, c, 0:128], xcb[:, c, sl],
                              start=(c == 0), stop=(c == NCH - 1), skip_group_check=True)
                    mm = tn.matmul(ps[4 + tt][0:32, :], xprojT_bf[:, c, 128:160], xcb[:, c, sl],
                                   start=(c == 0), stop=(c == NCH - 1), skip_group_check=True)
                return mm
            p.step("tensor", ["scalar"], xproj_mm)

        p.step("scalar", ["tensor"],
               lambda s: (s.copy(dt_bf[:, 0:512], ps[2][:]),
                          s.copy(dt_bf[:, 512:1024], ps[3][:]),
                          s.copy(bc_f[:, 0:512], ps[4][0:32, :]),
                          s.copy(bc_f[:, 512:1024], ps[5][0:32, :]),
                          s.copy(bc_bf[:, 0:512], ps[4][0:32, :]),
                          s.copy(bc_bf[:, 512:1024], ps[5][0:32, :]))[-1])



def _build_A_scan(p, L, stages=99):
    nc = L["nc"]; tB = L["tB"]; tC = L["tC"]; tD = L["tD"]; tE = L["tE"]; tA = L["tA"]
    buf0 = L["buf0"]; xcb = L["xcb"]; a_t = L["a_t"]; Asb = L["Asb"]
    dtprojT_bf = L["dtprojT_bf"]; dt_bf = L["dt_bf"]; dpb = L["dpb"]
    selB = L["selB"]; bc_bf = L["bc_bf"]; ps = L["ps"]; hfin = L["hfin"]
    if True:
        for c in range(NCH):
            def dtp_mm(tn, c=c):
                for tt in range(2):
                    mm = tn.matmul(ps[6 + tt][:], dtprojT_bf[:, c * 128:(c + 1) * 128],
                                   dt_bf[:, tt * 512:(tt + 1) * 512], start=True, stop=True)
                return mm
            p.step("tensor", ["scalar", "vector"], dtp_mm)
            p.step("scalar", ["tensor"],
                   lambda s, c=c: (s.activation(tB[:, 0:512], ps[6][:], AF.Identity, bias=dpb[:, c:c + 1]),
                                   s.activation(tB[:, 512:1024], ps[7][:], AF.Identity, bias=dpb[:, c:c + 1]))[-1])
            p.step("scalar", ["scalar"], lambda s: s.activation(tC[:], tB[:], AF.Abs))
            p.step("scalar", ["scalar"], lambda s: s.activation(tC[:], tC[:], AF.Exp, scale=-1.0))
            p.step("scalar", ["scalar"], lambda s: s.activation(tC[:], tC[:], AF.Ln, bias=1.0))
            p.step("vector", ["scalar"],
                   lambda v: v.tensor_scalar_max(tD[:], tB[:], 0.0))
            p.step("vector", [],
                   lambda v, c=c: v.tensor_add(buf0[:, c, :], tD[:], tC[:]))
            p.step("vector", ["vector"],
                   lambda v, c=c: v.tensor_mul(tD[:], buf0[:, c, :], xcb[:, c, :]))
            for n in range(NST):
                parn = n % 2
                p.step("scalar", ["vector"],
                       lambda s, c=c, n=n, parn=parn:
                       s.activation(a_t[parn][:], buf0[:, c, :], AF.Exp,
                                    scale=Asb[:, c, n:n + 1]))
                p.step("tensor", ["vector", "gpsimd", "scalar"],
                       lambda tn, n=n: (
                           tn.matmul(ps[0][:], selB[:, n, :], bc_bf[:, 0:512], start=True, stop=True),
                           tn.matmul(ps[1][:], selB[:, n, :], bc_bf[:, 512:1024], start=True, stop=True))[-1])
                def bmul_v(v, c=c, n=n):
                    v.tensor_mul(tE[:, 0:512], tD[:, 0:512], ps[0][:])
                    return v.tensor_mul(tE[:, 512:1024], tD[:, 512:1024], ps[1][:])
                p.step("vector", ["scalar", "tensor"], bmul_v)
                p.step("vector", [],
                       lambda v, parn=parn: v.tensor_tensor_scan(
                           tA[:], a_t[parn][:], tE[:], 0.0, ALU.mult, ALU.add))
                p.step("vector", ["vector"],
                       lambda v, c=c, n=n: v.tensor_copy(hfin[:, c * NST + n:c * NST + n + 1],
                                                         tA[:, TC - 1:TC]))



def _build_A_stores(p, L, stages):
    o_xn = L["o_xn"]; o_xc = L["o_xc"]; o_dl = L["o_dl"]; o_bc = L["o_bc"]
    o_hf = L["o_hf"]
    xn = L["xn"]; xcb = L["xcb"]; buf0 = L["buf0"]; bc_f = L["bc_f"]
    hfin = L["hfin"]
    if True:
        for f in [
            lambda s: s.dma_start(o_xn.ap().rearrange("(c p) t -> p c t", p=128), xn[:]),
            lambda s: s.dma_start(o_xc.ap().rearrange("(c p) t -> p c t", p=128), xcb[:]),
            lambda s: s.dma_start(o_dl.ap().rearrange("(c p) t -> p c t", p=128), buf0[:]),
            lambda s: s.dma_start(o_bc.ap(), bc_f[:]),
            lambda s: s.dma_start(o_hf.ap(), hfin[:]),
        ]:
            p.step("sync", ["vector", "scalar", "tensor"], f, inc=16)
        p.step("sync", ["sync"], lambda s: None, inc=0)




def build_B1(p_ext=None, IO=None, masked=False):
    if p_ext is not None:
        nc = p_ext.nc
    else:
        nc = bass.Bass("TRN2", target_bir_lowering=False)

    def T_in(name, shape, dtype):
        return IO[name] if IO is not None else nc.dram_tensor(name, shape, dtype, kind="ExternalInput")

    def T_out(name, shape, dtype):
        return IO[name] if IO is not None else nc.dram_tensor(name, shape, dtype, kind="ExternalOutput")
    i_dl = T_in("i_dl", [D, TC], F32)
    i_xc = T_in("i_xc", [D, TC], BF16)
    i_bc = T_in("i_bc", [32, TC], F32)
    A_log = T_in("A_log", [D, NST], F32)
    D_par = T_in("D_par", [128, NCH], F32)
    i_xn = T_in("i_xn", [D, TC], BF16)
    i_hi = T_in("i_hi", [128, NCH * NST], F32)
    o_s = T_out("o_s", [D, TC], F32)

    with ExitStack() as ctx:
        e = ctx.enter_context
        dl = e(nc.sbuf_tensor(_nm("dl"), [128, NCH, TC], F32))
        xcb = e(nc.sbuf_tensor(_nm("xcb"), [128, NCH, TC], BF16))
        sbuf = e(nc.sbuf_tensor(_nm("sbuf_s"), [128, NCH, TC], F32))
        tA = e(nc.sbuf_tensor(_nm("tA"), [128, TC], F32))
        tB = e(nc.sbuf_tensor(_nm("tB"), [128, TC], F32))
        tC = e(nc.sbuf_tensor(_nm("tC"), [128, TC], F32))
        tD = e(nc.sbuf_tensor(_nm("tD"), [128, TC], F32))
        tE = e(nc.sbuf_tensor(_nm("tE"), [128, TC], F32))
        bc_bf = e(nc.sbuf_tensor(_nm("bc_bf"), [32, TC], BF16))
        selB = e(nc.sbuf_tensor(_nm("selB"), [32, NST, 128], BF16))
        selC = e(nc.sbuf_tensor(_nm("selC"), [32, NST, 128], BF16))
        Asb = e(nc.sbuf_tensor(_nm("Asb"), [128, NCH, NST], F32))
        Dp = e(nc.sbuf_tensor(_nm("Dp"), [128, NCH], F32))
        hinit = e(nc.sbuf_tensor(_nm("hinit"), [128, NCH * NST], F32))
        xnp = [e(nc.sbuf_tensor(_nm(f"xnp{i}"), [128, TC], BF16)) for i in range(2)]
        ps = [e(nc.psum_tensor(_nm(f"ps{i}"), [128, 512], F32)) for i in range(4)]

        p = p_ext if p_ext is not None else Prog(nc, ctx)
        if p_ext is not None:
            p.fence()
        a_t = [tB, tC]

        for f in [
            lambda s: s.dma_start(dl[:], i_dl.ap().rearrange("(c p) t -> p c t", p=128)),
            lambda s: s.dma_start(xcb[:], i_xc.ap().rearrange("(c p) t -> p c t", p=128)),
            lambda s: s.dma_start(Asb[:], A_log.ap().rearrange("(c p) n -> p c n", p=128)),
            lambda s: s.dma_start(Dp[:], D_par.ap()),
            lambda s: s.dma_start(hinit[:], i_hi.ap()),
        ]:
            p.step("sync", [], f, inc=16)
        p.step("gpsimd", [], lambda g: g.dma_start(bc_bf[:], i_bc.ap()), inc=16)

        def consts(g):
            g.memset(selB[:], 0.0)
            return g.memset(selC[:], 0.0)
        p.step("gpsimd", [], consts)
        p.step("gpsimd", [], lambda g: g.affine_select(
            out=selB[:], in_=selB[:], compare_op=ALU.not_equal, fill=1.0, base=0,
            pattern=[[-1, NST], [0, 128]], channel_multiplier=1))
        p.step("gpsimd", [], lambda g: g.affine_select(
            out=selC[:], in_=selC[:], compare_op=ALU.not_equal, fill=1.0, base=-16,
            pattern=[[-1, NST], [0, 128]], channel_multiplier=1))

        p.step("scalar", ["sync"], lambda s: s.activation(Asb[:], Asb[:], AF.Exp))
        p.step("scalar", [], lambda s: s.mul(Asb[:], Asb[:], -1.0))
        p.step("vector", ["sync", "gpsimd", "scalar"], lambda v: None, inc=0)
        if masked:
            mskh = T_in("msk", [128, 1], F32)
            mskcol = e(nc.sbuf_tensor(_nm("mskcol"), [128, 1], F32))
            p.step("sync", [], lambda s: s.dma_start(mskcol[:], mskh.ap()), inc=16)
            p.step("vector", ["sync"], lambda v: v.tensor_scalar(
                hinit[:], hinit[:], mskcol[:], None, ALU.mult))

        for c in range(NCH):
            p.step("vector", [],
                   lambda v, c=c: v.tensor_mul(tD[:], dl[:, c, :], xcb[:, c, :]))
            for n in range(NST):
                parn = n % 2
                p.step("scalar", ["vector"],
                       lambda s, c=c, n=n, parn=parn:
                       s.activation(a_t[parn][:], dl[:, c, :], AF.Exp,
                                    scale=Asb[:, c, n:n + 1]))
                p.step("tensor", ["vector", "gpsimd", "scalar"],
                       lambda tn, n=n: (
                           tn.matmul(ps[0][:], selB[:, n, :], bc_bf[:, 0:512], start=True, stop=True),
                           tn.matmul(ps[1][:], selB[:, n, :], bc_bf[:, 512:1024], start=True, stop=True),
                           tn.matmul(ps[2][:], selC[:, n, :], bc_bf[:, 0:512], start=True, stop=True),
                           tn.matmul(ps[3][:], selC[:, n, :], bc_bf[:, 512:1024], start=True, stop=True))[-1])
                def bmul_v(v):
                    v.tensor_mul(tE[:, 0:512], tD[:, 0:512], ps[0][:])
                    return v.tensor_mul(tE[:, 512:1024], tD[:, 512:1024], ps[1][:])
                p.step("vector", ["scalar", "tensor"], bmul_v)
                idx = c * NST + n
                p.step("vector", [],
                       lambda v, parn=parn, idx=idx: v.tensor_tensor_scan(
                           tA[:], a_t[parn][:], tE[:], hinit[:, idx:idx + 1],
                           ALU.mult, ALU.add))
                def ymul_v(v):
                    v.tensor_mul(tE[:, 0:512], tA[:, 0:512], ps[2][:])
                    return v.tensor_mul(tE[:, 512:1024], tA[:, 512:1024], ps[3][:])
                p.step("vector", [], ymul_v)
                if n == 0:
                    p.step("vector", [],
                           lambda v, c=c: v.tensor_copy(sbuf[:, c, :], tE[:]))
                else:
                    p.step("vector", [],
                           lambda v, c=c: v.tensor_add(sbuf[:, c, :], sbuf[:, c, :], tE[:]))

        for c in range(NCH):
            par = c % 2
            xt = xnp[par]
            p.step("sync", ["scalar", "vector"],
                   lambda s, c=c, xt=xt: s.dma_start(
                       xt[:], i_xn.ap().rearrange("(c p) t -> p c t", p=128)[:, c, :]),
                   inc=16)
            p.step("vector", ["sync"],
                   lambda v, c=c, xt=xt: v.scalar_tensor_tensor(
                       sbuf[:, c, :], xt[:], Dp[:, c:c + 1], sbuf[:, c, :],
                       ALU.mult, ALU.add))

        p.step("sync", ["vector"],
               lambda s: s.dma_start(o_s.ap().rearrange("(c p) t -> p c t", p=128), sbuf[:]),
               inc=16)
        p.step("sync", ["sync"], lambda s: None, inc=0)

        if p_ext is None:
            block = e(nc.Block())
            p.emit(block)
    return nc


def _prep_B1_inmaps(inputs, resA):
    A_log = np.ascontiguousarray(inputs["L0_attn_A_log"])
    D_par = np.ascontiguousarray(inputs["L0_attn_D_param"].reshape(NCH, 128).T)
    x = inputs["L0_attn_x"]
    zeros_h = np.zeros((128, NCH * NST), np.float32)
    maps = []
    for c in range(8):
        b, h = c // 2, c % 2
        t0 = h * TC
        hi = zeros_h if h == 0 else np.ascontiguousarray(resA[c - 1]["o_hf"])
        maps.append(dict(i_dl=resA[c]["o_dl"], i_xc=resA[c]["o_xc"],
                         i_bc=resA[c]["o_bc"], A_log=A_log, D_par=D_par,
                         i_xn=resA[c]["o_xn"], i_hi=hi))
    return maps


def run_B1(inputs, resA):
    nc = build_B1()
    res = run_bass_kernel_spmd(nc, _prep_B1_inmaps(inputs, resA), core_ids=list(range(8)))
    return res.results




def build_B2(p_ext=None, IO=None):
    if p_ext is not None:
        nc = p_ext.nc
    else:
        nc = bass.Bass("TRN2", target_bir_lowering=False)

    def T_in(name, shape, dtype):
        return IO[name] if IO is not None else nc.dram_tensor(name, shape, dtype, kind="ExternalInput")

    def T_out(name, shape, dtype):
        return IO[name] if IO is not None else nc.dram_tensor(name, shape, dtype, kind="ExternalOutput")

    i_xn = T_in("i_xn", [D, TC], BF16)
    i_s = T_in("i_s", [D, TC], F32)
    xT = T_in("xT", [D, TC], F32)
    w1_all = T_in("w1_all", [NEXP, D, DFF], F32)
    w2_all = T_in("w2_all", [NEXP, DFF, D], F32)
    b1_all = T_in("b1_all", [128, NEXP, NFC], F32)
    sh_b2 = T_in("sh_b2", [128, 2, NCH], F32)
    rt_b2 = T_in("rt_b2", [16, D], F32)
    cenT = T_in("cenT", [D, 16], F32)
    cen_bias = T_in("cen_bias", [1, 16], F32)
    o_x1 = T_out("o_x1", [D, TC], F32)

    with ExitStack() as ctx:
        e = ctx.enter_context
        xn = e(nc.sbuf_tensor(_nm("xn"), [128, NCH, TC], BF16))
        y_sb = e(nc.sbuf_tensor(_nm("y_sb"), [128, NCH, 512], F32))
        gt_bf = e(nc.sbuf_tensor(_nm("gt_bf"), [32, TC], BF16))
        g_bf = e(nc.sbuf_tensor(_nm("g_bf"), [128, TC], BF16))
        selB = e(nc.sbuf_tensor(_nm("selB"), [32, NST, 128], BF16))
        cen_sb = e(nc.sbuf_tensor(_nm("cen_sb"), [128, NCH, 16], BF16))
        b1sb = e(nc.sbuf_tensor(_nm("b1sb"), [128, NEXP, NFC], F32))
        shb2 = e(nc.sbuf_tensor(_nm("shb2"), [128, 2, NCH], F32))
        shb2s = e(nc.sbuf_tensor(_nm("shb2s"), [128, NCH], F32))
        rtb2_bf = e(nc.sbuf_tensor(_nm("rtb2_bf"), [16, NCH, 128], BF16))
        ident_f = e(nc.sbuf_tensor(_nm("ident_f"), [128, 128], F32))
        ones_rf = e(nc.sbuf_tensor(_nm("ones_rf"), [1, 128], F32))
        bias_b = e(nc.sbuf_tensor(_nm("bias_b"), [128, 16], F32))
        sc_f = e(nc.sbuf_tensor(_nm("sc_f"), [128, 16], F32))
        sc_c = e(nc.sbuf_tensor(_nm("sc_c"), [128, 16], F32))
        ge_f = e(nc.sbuf_tensor(_nm("ge_f"), [128, 16], F32))
        m_f = e(nc.sbuf_tensor(_nm("m_f"), [128, 1], F32))
        sum_f = e(nc.sbuf_tensor(_nm("sum_f"), [128, 1], F32))
        g_tok = e(nc.sbuf_tensor(_nm("g_tok"), [128, 16], F32))
        x_t = e(nc.sbuf_tensor(_nm("x_t"), [128, TC], F32))
        s_t = e(nc.sbuf_tensor(_nm("s_t"), [128, TC], F32))
        g_t = e(nc.sbuf_tensor(_nm("g_t"), [128, TC], F32))
        w1g = [e(nc.sbuf_tensor(_nm(f"w1g{i}"), [128, NCH, 768], BF16)) for i in range(2)]
        w2sb = [e(nc.sbuf_tensor(_nm(f"w2sb{i}"), [128, NFC, D], BF16)) for i in range(1)]
        h1 = e(nc.sbuf_tensor(_nm("h1"), [128, NFC, 512], BF16))
        cbrow = e(nc.sbuf_tensor(_nm("cbrow"), [1, 16], F32))
        ps = [e(nc.psum_tensor(_nm(f"ps{i}"), [128, 512], F32)) for i in range(8)]

        p = p_ext if p_ext is not None else Prog(nc, ctx)
        if p_ext is not None:
            p.fence()

        for f in [
            lambda s: s.dma_start(xn[:], i_xn.ap().rearrange("(c p) t -> p c t", p=128)),
            lambda s: s.dma_start(b1sb[:], b1_all.ap()),
            lambda s: s.dma_start(shb2[:], sh_b2.ap()),
            lambda s: s.dma_start(cbrow[:], cen_bias.ap()),
        ]:
            p.step("sync", [], f, inc=16)
        p.step("gpsimd", [], lambda g: g.dma_start(
            cen_sb[:], cenT.ap().rearrange("(c p) j -> p c j", p=128)), inc=16)
        p.step("gpsimd", [], lambda g: g.dma_start(
            rtb2_bf[:], rt_b2.ap().rearrange("j (c p) -> j c p", p=128)), inc=16)

        def consts(g):
            g.memset(selB[:], 0.0)
            g.memset(gt_bf[:], 0.0)
            g.memset(ident_f[:], 0.0)
            return g.memset(ones_rf[:], 1.0)
        p.step("gpsimd", ["gpsimd"], consts)
        p.step("gpsimd", ["gpsimd"], lambda g: g.affine_select(
            out=selB[:], in_=selB[:], compare_op=ALU.not_equal, fill=1.0, base=0,
            pattern=[[-1, NST], [0, 128]], channel_multiplier=1))
        p.step("gpsimd", ["gpsimd"], lambda g: g.affine_select(
            out=ident_f[:], in_=ident_f[:], compare_op=ALU.not_equal, fill=1.0, base=0,
            pattern=[[-1, 128]], channel_multiplier=1))
        p.step("tensor", ["sync", "gpsimd"],
               lambda tn: tn.matmul(ps[1][:, 0:16], ones_rf[:], cbrow[:], start=True, stop=True))
        p.step("scalar", ["tensor"], lambda s: s.copy(bias_b[:], ps[1][:, 0:16]))

        for tile in range(8):
            tsl = slice(tile * 128, (tile + 1) * 128)
            def sc_mm(tn, tsl=tsl):
                for c in range(NCH):
                    mm = tn.matmul(ps[0][:, 0:16], xn[:, c, tsl], cen_sb[:, c, :],
                                   start=(c == 0), stop=(c == NCH - 1))
                return mm
            p.step("tensor", ["sync", "gpsimd", ("scalar", p.n("scalar"))], sc_mm)
            p.step("scalar", ["tensor"], lambda s: s.activation(sc_f[:], ps[0][:, 0:16], AF.Sigmoid))
            p.step("vector", ["scalar"], lambda v: v.tensor_add(sc_f[:], sc_f[:], bias_b[:]))
            p.step("vector", [], lambda v: v.tensor_copy(sc_c[:], sc_f[:]))
            for i in range(4):
                p.step("vector", [], lambda v: v.reduce_max(m_f[:], sc_c[:], axis=AX.X))
                if i < 3:
                    p.step("vector", [], lambda v: v.tensor_scalar(ge_f[:], sc_c[:], m_f[:], None, ALU.is_ge))
                    p.step("vector", [], lambda v: v.scalar_tensor_tensor(
                        sc_c[:], ge_f[:], -3.0e38, sc_c[:], ALU.mult, ALU.add))
            p.step("vector", [], lambda v: v.tensor_scalar(ge_f[:], sc_f[:], m_f[:], None, ALU.is_ge))
            p.step("vector", [], lambda v: v.tensor_mul(g_tok[:], ge_f[:], sc_f[:]))
            p.step("vector", [], lambda v: v.reduce_sum(sum_f[:], g_tok[:], axis=AX.X))
            p.step("vector", [], lambda v: v.tensor_scalar_max(sum_f[:], sum_f[:], 1e-8))
            p.step("vector", [], lambda v: v.reciprocal(m_f[:], sum_f[:]))
            p.step("vector", [], lambda v: v.tensor_scalar(g_tok[:], g_tok[:], m_f[:], None, ALU.mult))
            p.step("tensor", ["vector", ("scalar", p.n("scalar"))],
                   lambda tn: tn.transpose(ps[1][0:16, 0:128], g_tok[:], ident_f[:]))
            p.step("scalar", ["tensor"],
                   lambda s, tsl=tsl: s.copy(gt_bf[0:16, tsl], ps[1][0:16, 0:128]))

        GR2 = [(0, 6), (6, 5)]
        w1_tasks = [(hf, j, gi) for hf in range(2) for j in range(NEXP) for gi in range(2)]
        w1_after = {}
        w1_reader = {}
        w2_after = {}
        w2_reader = {}

        def w1_dma(k):
            hf, j, gi = w1_tasks[k]
            f0, nf = GR2[gi]
            par = k % 2
            waits = []
            if k >= 2:
                waits.append(("tensor", w1_reader[k - 2]))
            p.step("gpsimd", waits,
                   lambda g, j=j, f0=f0, nf=nf, par=par: g.dma_start(
                       w1g[par][:, :, 0:nf * 128],
                       w1_all.ap()[j, :, f0 * 128:(f0 + nf) * 128]
                       .rearrange("(c p) f -> p c f", p=128)),
                   inc=16)
            w1_after[k] = p.n("gpsimd")

        def w2_dma(hf, j, rd=None):
            waits = [] if rd is None else [("tensor", rd)]
            p.step("gpsimd", waits,
                   lambda g, j=j: g.dma_start(
                       w2sb[0][:], w2_all.ap()[j].rearrange("(f p) d -> p f d", p=128)),
                   inc=16)
            w2_after[(hf, j)] = p.n("gpsimd")

        w1_dma(0)
        w1_dma(1)
        w2_dma(0, 0)

        for hf in range(2):
            hsl = slice(hf * 512, (hf + 1) * 512)
            for j in range(NEXP):
                k0 = (hf * NEXP + j) * 2
                for gi, (f0, nf) in enumerate(GR2):
                    k = k0 + gi
                    par = k % 2
                    def h1_mm(tn, f0=f0, nf=nf, par=par, hsl=hsl):
                        for c in range(NCH):
                            for q in range(nf):
                                mm = tn.matmul(ps[q][:], w1g[par][:, c, q * 128:(q + 1) * 128],
                                               xn[:, c, hsl], start=(c == 0),
                                               stop=(c == NCH - 1), skip_group_check=True)
                        return mm
                    p.step("tensor", [("gpsimd", w1_after[k]),
                                      ("scalar", p.n("scalar"))], h1_mm)
                    w1_reader[k] = p.n("tensor")
                    if k + 2 < len(w1_tasks):
                        w1_dma(k + 2)
                    def h1_ev(s, j=j, f0=f0, nf=nf):
                        for q in range(nf):
                            ev = s.activation(h1[:, f0 + q, :], ps[q][:], AF.Relu,
                                              bias=b1sb[:, j, f0 + q:f0 + q + 1])
                        return ev
                    p.step("scalar", ["tensor"], h1_ev)
                if j >= 2:
                    def gb_mm(tn, j=j, hsl=hsl):
                        return tn.matmul(ps[6][:], selB[:, j - 2, :], gt_bf[:, hsl],
                                         start=True, stop=True)
                    p.step("tensor", ["vector", ("scalar", p.n("scalar"))], gb_mm)
                    p.step("scalar", ["tensor"],
                           lambda s: s.copy(g_bf[:, 0:512], ps[6][:]))
                    for q in range(NFC):
                        p.step("vector", ["scalar"],
                               lambda v, q=q: v.tensor_mul(h1[:, q, :], h1[:, q, :], g_bf[:, 0:512]))
                for c in range(NCH):
                    pary = c % 2
                    def y_mm(tn, c=c, pary=pary):
                        for q in range(NFC):
                            mm = tn.matmul(ps[6 + pary][:], w2sb[0][:, q, c * 128:(c + 1) * 128],
                                           h1[:, q, :], start=(q == 0), stop=(q == NFC - 1))
                        return mm
                    p.step("tensor", [("gpsimd", w2_after[(hf, j)]), "vector",
                                      ("scalar", p.n("scalar"))], y_mm)
                    if j == 0:
                        p.step("vector", ["tensor"],
                               lambda v, c=c, pary=pary: v.tensor_copy(
                                   y_sb[:, c, :], ps[6 + pary][:]))
                    else:
                        p.step("vector", ["tensor"],
                               lambda v, c=c, pary=pary: v.tensor_add(
                                   y_sb[:, c, :], y_sb[:, c, :], ps[6 + pary][:]))
                w2_reader[(hf, j)] = p.n("tensor")
                if j + 1 < NEXP:
                    w2_dma(hf, j + 1, rd=w2_reader[(hf, j)])
                elif hf == 0:
                    w2_dma(1, 0, rd=w2_reader[(hf, j)])

            if hf == 0:
                p.step("vector", ["sync"], lambda v: v.tensor_add(
                    shb2s[:], shb2[:, 0, :], shb2[:, 1, :]))
            for c in range(NCH):
                pary = c % 2
                p.step("tensor", ["vector"],
                       lambda tn, c=c, pary=pary, hsl=hsl: tn.matmul(
                           ps[6 + pary][:], rtb2_bf[:, c, :], gt_bf[0:16, hsl],
                           start=True, stop=True))
                p.step("vector", ["tensor"],
                       lambda v, c=c, pary=pary: v.tensor_add(
                           y_sb[:, c, :], y_sb[:, c, :], ps[6 + pary][:]))
            for c in range(NCH):
                p.step("sync", [("vector", p.n("vector")), ("scalar", p.n("scalar"))],
                       lambda s, c=c, hsl=hsl: s.dma_start(
                           x_t[:, 0:512],
                           xT.ap().rearrange("(c p) t -> p c t", p=128)[:, c, hsl]),
                       inc=16)
                p.step("sync", [],
                       lambda s, c=c, hsl=hsl: s.dma_start(
                           s_t[:, 0:512],
                           i_s.ap().rearrange("(c p) t -> p c t", p=128)[:, c, hsl]),
                       inc=16)
                p.step("vector", ["sync"],
                       lambda v, c=c, hsl=hsl: v.tensor_add(
                           y_sb[:, c, :], y_sb[:, c, :], xn[:, c, hsl]))
                p.step("vector", [],
                       lambda v, c=c: v.tensor_scalar(y_sb[:, c, :], y_sb[:, c, :],
                                                      shb2s[:, c:c + 1], None, ALU.add))
                p.step("scalar", ["vector", ("sync", p.n("sync"))],
                       lambda s, c=c: s.activation(g_t[:, 0:512], y_sb[:, c, :], AF.Silu))
                p.step("vector", ["scalar"],
                       lambda v: v.tensor_mul(g_t[:, 0:512], g_t[:, 0:512], s_t[:, 0:512]))
                p.step("vector", [], lambda v: v.tensor_add(g_t[:, 0:512], g_t[:, 0:512], x_t[:, 0:512]))
                p.step("sync", ["vector"],
                       lambda s, c=c, hsl=hsl: s.dma_start(
                           o_x1.ap().rearrange("(c p) t -> p c t", p=128)[:, c, hsl],
                           g_t[:, 0:512]),
                       inc=16)
        p.step("sync", ["sync"], lambda s: None, inc=0)

        if p_ext is None:
            block = e(nc.Block())
            p.emit(block)
    return nc


def _prep_B2_inmaps(inputs, resA, resB1):
    w1_all = np.ascontiguousarray(np.concatenate(
        [inputs["L0_ffn_shared_w1"], inputs["L0_ffn_routed_w1"]], 0))
    w2_all = np.ascontiguousarray(np.concatenate(
        [inputs["L0_ffn_shared_w2"], inputs["L0_ffn_routed_w2"]], 0))
    b1_all = np.ascontiguousarray(np.concatenate(
        [inputs["L0_ffn_shared_b1"], inputs["L0_ffn_routed_b1"]], 0)
        .reshape(NEXP, NFC, 128).transpose(2, 0, 1))
    sh_b2 = np.ascontiguousarray(
        inputs["L0_ffn_shared_b2"].reshape(2, NCH, 128).transpose(2, 0, 1))
    rt_b2 = np.ascontiguousarray(inputs["L0_ffn_routed_b2"])
    cenT = np.ascontiguousarray(inputs["L0_ffn_centroids"].T)
    cen_bias = np.ascontiguousarray(inputs["L0_ffn_bias"].reshape(1, 16))
    x = inputs["L0_attn_x"]
    maps = []
    for c in range(8):
        b, h = c // 2, c % 2
        t0 = h * TC
        maps.append(dict(i_xn=resA[c]["o_xn"], i_s=resB1[c]["o_s"],
                         xT=_fm(x[b, t0:t0 + TC]),
                         w1_all=w1_all, w2_all=w2_all, b1_all=b1_all,
                         sh_b2=sh_b2, rt_b2=rt_b2, cenT=cenT, cen_bias=cen_bias))
    return maps


def run_B2(inputs, resA, resB1):
    nc = build_B2()
    res = run_bass_kernel_spmd(nc, _prep_B2_inmaps(inputs, resA, resB1), core_ids=list(range(8)))
    return res.results




def build_C(p_ext=None, IO=None):
    if p_ext is not None:
        nc = p_ext.nc
    else:
        nc = bass.Bass("TRN2", target_bir_lowering=False)

    def T_in(name, shape, dtype):
        return IO[name] if IO is not None else nc.dram_tensor(name, shape, dtype, kind="ExternalInput")

    def T_out(name, shape, dtype):
        return IO[name] if IO is not None else nc.dram_tensor(name, shape, dtype, kind="ExternalOutput")
    x1a = T_in("x1a", [D, TC], F32)
    x1b = T_in("x1b", [D, TC], F32)
    wqT = T_in("wqT", [D, NH * HD], F32)
    wkT = T_in("wkT", [D, NKV * HD], F32)
    wvT = T_in("wvT", [D, NKV * HD], F32)
    o_y = T_out("o_y", [NH * HD, T], BF16)

    LOG1E4 = float(np.log(10000.0))
    TWO_PI = float(2 * np.pi)

    with ExitStack() as ctx:
        e = ctx.enter_context
        xn1 = e(nc.sbuf_tensor(_nm("xn1"), [128, NCH, T], BF16))
        q_fm = e(nc.sbuf_tensor(_nm("q_fm"), [128, 8, T], BF16))
        k_fm = e(nc.sbuf_tensor(_nm("k_fm"), [128, 2, T], BF16))
        v_aug = e(nc.sbuf_tensor(_nm("v_aug"), [128, NST, NKV * 65], BF16))
        cos_q = e(nc.sbuf_tensor(_nm("cos_q"), [128, NST, 64], BF16))
        sin_q = e(nc.sbuf_tensor(_nm("sin_q"), [128, NST, 64], BF16))
        cos_k = e(nc.sbuf_tensor(_nm("cos_k"), [128, NST, 64], BF16))
        sin_k = e(nc.sbuf_tensor(_nm("sin_k"), [128, NST, 64], BF16))
        rstd_b = e(nc.sbuf_tensor(_nm("rstd_b"), [128, T], F32))
        nmr_b = e(nc.sbuf_tensor(_nm("nmr_b"), [128, T], F32))
        rm = e(nc.sbuf_tensor(_nm("rm"), [1, 512], F32))
        rv = e(nc.sbuf_tensor(_nm("rv"), [1, 512], F32))
        rn = e(nc.sbuf_tensor(_nm("rn"), [1, 512], F32))
        sq5 = e(nc.sbuf_tensor(_nm("sq5"), [128, 512], F32))
        xc_t = [e(nc.sbuf_tensor(_nm("xc_t0"), [128, T], F32))] * 2
        wq_sb = e(nc.sbuf_tensor(_nm("wq_sb"), [128, NCH, 512], BF16))
        wk_sb = e(nc.sbuf_tensor(_nm("wk_sb"), [128, NCH, NKV * HD], BF16))
        wv_sb = e(nc.sbuf_tensor(_nm("wv_sb"), [128, NCH, NKV * HD], BF16))
        probs = [e(nc.sbuf_tensor(_nm(f"probs{i}"), [128, 512], BF16)) for i in range(2)]
        M0 = e(nc.sbuf_tensor(_nm("M0"), [128, 896], BF16))
        y_f32 = e(nc.sbuf_tensor(_nm("y_f32"), [65, 512], F32))
        ybuf = e(nc.sbuf_tensor(_nm("ybuf"), [64, 512], BF16))
        q_raw = e(nc.sbuf_tensor(_nm("q_raw"), [128, 512], F32))
        rope1 = e(nc.sbuf_tensor(_nm("rope1"), [128, 512], F32))
        rope2 = e(nc.sbuf_tensor(_nm("rope2"), [128, 512], F32))
        q_tok = e(nc.sbuf_tensor(_nm("q_tok"), [128, 512], BF16))
        fr = [e(nc.sbuf_tensor(_nm(f"fr{i}"), [128, 32], F32)) for i in range(4)]
        fri = e(nc.sbuf_tensor(_nm("fri"), [128, 32], I32))
        t_row = e(nc.sbuf_tensor(_nm("t_row"), [1, 128], F32))
        th_row = e(nc.sbuf_tensor(_nm("th_row"), [1, 32], F32))
        ones_all = e(nc.sbuf_tensor(_nm("ones_all"), [128, 128], F32))
        ident_f = e(nc.sbuf_tensor(_nm("ident_f"), [128, 128], F32))
        eps_col = e(nc.sbuf_tensor(_nm("eps_col"), [128, 1], F32))
        shift_col = e(nc.sbuf_tensor(_nm("shift_col"), [128, 1], F32))
        ps = [e(nc.psum_tensor(_nm(f"ps{i}"), [128, 512], F32)) for i in range(8)]

        p = p_ext if p_ext is not None else Prog(nc, ctx)
        if p_ext is not None:
            p.fence()
        rows = {"m": rm, "var": rv, "mm": rn, "rstd": rv, "nmr": rn}

        def consts(g):
            g.memset(ones_all[:], 1.0)
            g.memset(ident_f[:], 0.0)
            g.memset(eps_col[:], LN_EPS)
            g.memset(shift_col[:], EXP_SHIFT)
            g.memset(M0[:], 0.0)
            return g.memset(
                v_aug[:].rearrange("p t (k u) -> p t k u", u=65)[:, :, :, 64:65], 1.0)
        p.step("gpsimd", [], consts)
        p.step("gpsimd", [], lambda g: g.affine_select(
            out=ident_f[:], in_=ident_f[:], compare_op=ALU.not_equal, fill=1.0, base=0,
            pattern=[[-1, 128]], channel_multiplier=1))
        p.step("gpsimd", [], lambda g: g.affine_select(
            out=M0[:], in_=M0[:], compare_op=ALU.is_ge, fill=-1e30, base=-384,
            pattern=[[1, 896]], channel_multiplier=-1))
        p.step("gpsimd", [], lambda g: g.dma_start(
            wk_sb[:], wkT.ap().rearrange("(c p) f -> p c f", p=128)), inc=16)
        p.step("gpsimd", [], lambda g: g.dma_start(
            wv_sb[:], wvT.ap().rearrange("(c p) f -> p c f", p=128)), inc=16)
        p.step("gpsimd", [], lambda g: g.iota(
            th_row[:], pattern=[[1, 32]], base=0, channel_multiplier=0,
            allow_small_or_imprecise_dtypes=True))
        p.step("scalar", ["gpsimd"], lambda s: s.activation(
            th_row[:], th_row[:], AF.Exp, scale=-LOG1E4 / 32.0))

        for c in range(NCH):
            par = c % 2
            p.step("sync", [("tensor", p.n("tensor")), ("scalar", p.n("scalar"))],
                   lambda s, c=c, par=par: s.dma_start(
                       xc_t[par][:, 0:TC],
                       x1a.ap().rearrange("(c p) t -> p c t", p=128)[:, c, :]),
                   inc=16)
            p.step("sync", [],
                   lambda s, c=c, par=par: s.dma_start(
                       xc_t[par][:, TC:T],
                       x1b.ap().rearrange("(c p) t -> p c t", p=128)[:, c, :]),
                   inc=16)
            for tt in range(4):
                sl = slice(tt * 512, (tt + 1) * 512)
                p.step("scalar", ["sync", "tensor"],
                       lambda s, c=c, par=par, sl=sl: s.activation(
                           sq5[:], xc_t[par][:, sl], AF.Square))
                p.step("tensor", ["scalar", "sync"],
                       lambda tn, c=c, par=par, sl=sl, tt=tt: (
                           tn.matmul(ps[tt][0:1, :], ones_all[:, 0:1], xc_t[par][:, sl],
                                     start=(c == 0), stop=(c == NCH - 1),
                                     skip_group_check=True),
                           tn.matmul(ps[4 + tt][0:1, :], ones_all[:, 0:1], sq5[:],
                                     start=(c == 0), stop=(c == NCH - 1),
                                     skip_group_check=True))[-1])
        for tt in range(4):
            sl = slice(tt * 512, (tt + 1) * 512)
            def fin_v(v, tt=tt):
                v.tensor_scalar_mul(rm[:], ps[tt][0:1, :], 1.0 / D)
                return v.tensor_scalar_mul(rv[:], ps[4 + tt][0:1, :], 1.0 / D)
            p.step("vector", ["tensor"], fin_v)
            p.step("vector", [], lambda v: v.tensor_mul(rn[:], rm[:], rm[:]))
            p.step("vector", [], lambda v: v.tensor_sub(rv[:], rv[:], rn[:]))
            p.step("scalar", ["vector"], lambda s: s.activation(rv[:], rv[:], AF.Ln, bias=eps_col[0:1, :]))
            p.step("scalar", [], lambda s: s.activation(rv[:], rv[:], AF.Exp, scale=-0.5))
            p.step("vector", ["scalar"], lambda v: v.scalar_tensor_tensor(
                rn[:], rm[:], -1.0, rv[:], ALU.mult, ALU.mult))
            p.step("tensor", ["vector"],
                   lambda tn, tt=tt: (
                       tn.matmul(ps[tt][:], ones_all[0:1, :], rv[:], start=True, stop=True),
                       tn.matmul(ps[4 + tt][:], ones_all[0:1, :], rn[:], start=True, stop=True))[-1])
            p.step("scalar", ["tensor"],
                   lambda s, tt=tt, sl=sl: (
                       s.copy(rstd_b[:, sl], ps[tt][:]),
                       s.copy(nmr_b[:, sl], ps[4 + tt][:]))[-1])
        for c in range(NCH):
            par = c % 2
            p.step("sync", [("vector", p.n("vector"))],
                   lambda s, c=c, par=par: s.dma_start(
                       xc_t[par][:, 0:TC],
                       x1a.ap().rearrange("(c p) t -> p c t", p=128)[:, c, :]),
                   inc=16)
            p.step("sync", [],
                   lambda s, c=c, par=par: s.dma_start(
                       xc_t[par][:, TC:T],
                       x1b.ap().rearrange("(c p) t -> p c t", p=128)[:, c, :]),
                   inc=16)
            for tt4 in range(4):
                s4 = slice(tt4 * 512, (tt4 + 1) * 512)
                p.step("vector", ["sync", "scalar"],
                       lambda v, c=c, par=par, s4=s4: v.tensor_mul(
                           ps[5][:], xc_t[par][:, s4], rstd_b[:, s4]))
                p.step("vector", [],
                       lambda v, c=c, s4=s4: v.tensor_add(
                           xn1[:, c, s4], ps[5][:], nmr_b[:, s4]))

        for tile in range(NST):
            p.step("gpsimd", [("tensor", p.n("tensor"))], lambda g, tile=tile: g.iota(
                t_row[:], pattern=[[1, 128]], base=tile * 128, channel_multiplier=0,
                allow_small_or_imprecise_dtypes=True))
            p.step("tensor", ["gpsimd", "scalar", ("vector", p.n("vector"))],
                   lambda tn: tn.matmul(ps[2][:, 0:32], t_row[:], th_row[:],
                                        start=True, stop=True))
            p.step("scalar", ["tensor"], lambda s: s.copy(fr[0][:], ps[2][:, 0:32]))
            p.step("vector", ["scalar"], lambda v: v.tensor_scalar_mul(fr[1][:], fr[0][:], 1.0 / TWO_PI))
            p.step("vector", [], lambda v: v.tensor_copy(fri[:], fr[1][:]))
            p.step("vector", [], lambda v: v.tensor_copy(fr[1][:], fri[:]))
            p.step("vector", [], lambda v: v.scalar_tensor_tensor(
                fr[2][:], fr[1][:], -TWO_PI, fr[0][:], ALU.mult, ALU.add))
            p.step("scalar", ["vector"], lambda s: s.activation(fr[0][:], fr[2][:], AF.Sin))
            p.step("vector", ["scalar"], lambda v: v.tensor_scalar_add(
                fr[3][:], fr[2][:], float(np.pi / 2)))
            p.step("vector", [], lambda v: v.tensor_scalar(
                fr[1][:], fr[3][:], float(np.pi), None, ALU.is_gt))
            p.step("vector", [], lambda v: v.scalar_tensor_tensor(
                fr[3][:], fr[1][:], -TWO_PI, fr[3][:], ALU.mult, ALU.add))
            p.step("scalar", ["vector"], lambda s: s.activation(fr[1][:], fr[3][:], AF.Sin))
            def wtab(v, tile=tile):
                v.tensor_scalar_mul(cos_q[:, tile, 0:32], fr[1][:], 0.125)
                v.tensor_scalar_mul(cos_q[:, tile, 32:64], fr[1][:], 0.125)
                v.tensor_scalar_mul(sin_q[:, tile, 0:32], fr[0][:], -0.125)
                v.tensor_scalar_mul(sin_q[:, tile, 32:64], fr[0][:], 0.125)
                v.tensor_scalar_mul(cos_k[:, tile, 0:32], fr[1][:], 1.0)
                v.tensor_scalar_mul(cos_k[:, tile, 32:64], fr[1][:], 1.0)
                v.tensor_scalar_mul(sin_k[:, tile, 0:32], fr[0][:], -1.0)
                return v.tensor_scalar_mul(sin_k[:, tile, 32:64], fr[0][:], 1.0)
            p.step("vector", ["scalar"], wtab)

        def rep_view(tab, tile, nb):
            th = tab.tensor if hasattr(tab, "tensor") else tab
            return bass.AP(th, tile * 64, [[NST * 64, 128], [0, nb], [1, 64]])

        def swap_view(buf, nb):
            th = buf.tensor if hasattr(buf, "tensor") else buf
            return bass.AP(th, 32, [[512, 128], [64, nb], [-32, 2], [1, 32]])

        for tkt in range(NST):
            tsl = slice(tkt * 128, (tkt + 1) * 128)
            def kv_mm(tn, tsl=tsl):
                for c in range(NCH):
                    tn.matmul(ps[0][:, 0:NKV * HD], xn1[:, c, tsl], wk_sb[:, c, :],
                              start=(c == 0), stop=(c == NCH - 1), skip_group_check=True)
                    mm = tn.matmul(ps[1][:, 0:NKV * HD], xn1[:, c, tsl], wv_sb[:, c, :],
                                   start=(c == 0), stop=(c == NCH - 1), skip_group_check=True)
                return mm
            p.step("tensor", ["gpsimd", "scalar", ("vector", p.n("vector"))], kv_mm)
            p.step("scalar", ["tensor"], lambda s: s.copy(q_raw[:, 0:NKV * HD], ps[0][:, 0:NKV * HD]))
            def v_ev(s, tkt=tkt):
                for kv in range(NKV):
                    ev = s.copy(v_aug[:, tkt, kv * 65:kv * 65 + 64],
                                ps[1][:, kv * HD:(kv + 1) * HD])
                return ev
            p.step("scalar", [], v_ev)
            p.step("vector", ["scalar"],
                   lambda v, tkt=tkt: v.tensor_mul(
                       rope1[:, 0:256].rearrange("p (b u) -> p b u", b=NKV),
                       q_raw[:, 0:256].rearrange("p (b u) -> p b u", b=NKV),
                       rep_view(cos_k, tkt, NKV)))
            p.step("vector", [],
                   lambda v, tkt=tkt: v.tensor_mul(
                       rope2[:, 0:256].rearrange("p (b u) -> p b u", b=NKV),
                       swap_view(q_raw, NKV),
                       rep_view(sin_k, tkt, NKV)))
            p.step("vector", [],
                   lambda v: v.tensor_add(q_raw[:, 0:256], rope1[:, 0:256], rope2[:, 0:256]))
            for kv in range(NKV):
                p.step("tensor", ["vector", "scalar"],
                       lambda tn, kv=kv: tn.transpose(
                           ps[2][0:HD, 0:128],
                           q_raw[:, kv * HD:(kv + 1) * HD], ident_f[:]))
                p.step("scalar", ["tensor"],
                       lambda s, kv=kv, tsl=tsl: s.copy(
                           k_fm[64 * (kv % 2):64 * (kv % 2) + 64, kv // 2, tsl],
                           ps[2][0:HD, 0:128]))

        for ft in range(2):
            p.step("gpsimd", [("tensor", p.n("tensor"))],
                   lambda g, ft=ft: g.dma_start(
                       wq_sb[:], wqT.ap()[:, ft * 512:(ft + 1) * 512]
                       .rearrange("(c p) f -> p c f", p=128)),
                   inc=16)
            for tqt in range(NST):
                tsl = slice(tqt * 128, (tqt + 1) * 128)
                def q_mm(tn, tsl=tsl):
                    for c in range(NCH):
                        mm = tn.matmul(ps[0][:], xn1[:, c, tsl], wq_sb[:, c, :],
                                       start=(c == 0), stop=(c == NCH - 1))
                    return mm
                p.step("tensor", ["gpsimd", "scalar", ("vector", p.n("vector"))], q_mm)
                p.step("scalar", ["tensor"], lambda s: s.copy(q_raw[:], ps[0][:]))
                p.step("vector", ["scalar"],
                       lambda v, tqt=tqt: v.tensor_mul(
                           rope1[:].rearrange("p (b u) -> p b u", b=8),
                           q_raw[:].rearrange("p (b u) -> p b u", b=8),
                           rep_view(cos_q, tqt, 8)))
                p.step("vector", [],
                       lambda v, tqt=tqt: v.tensor_mul(
                           rope2[:].rearrange("p (b u) -> p b u", b=8),
                           swap_view(q_raw, 8),
                           rep_view(sin_q, tqt, 8)))
                p.step("vector", [],
                       lambda v: v.tensor_add(q_raw[:], rope1[:], rope2[:]))
                for hh in range(8):
                    h = ft * 8 + hh
                    qb = 64 * ((h // 4) % 2)
                    qs = (h % 4) + 4 * (h // 8)
                    p.step("tensor", ["vector", "scalar"],
                           lambda tn, hh=hh: tn.transpose(
                               ps[1][0:HD, 0:128],
                               q_raw[:, hh * HD:(hh + 1) * HD], ident_f[:]))
                    p.step("scalar", ["tensor"],
                           lambda s, qb=qb, qs=qs, tsl=tsl: s.copy(
                               q_fm[qb:qb + 64, qs, tsl],
                               ps[1][0:HD, 0:128]))

        for h in range(NH):
            kv = h // 4
            qb = 64 * ((h // 4) % 2)
            qs = (h % 4) + 4 * (h // 8)
            kb = 64 * (kv % 2)
            for tq4 in range(4):
                qsl = slice(tq4 * 512, (tq4 + 1) * 512)
                ntk = 4 * (tq4 + 1)
                for tkt in range(ntk):
                    par = tkt % 2
                    p.step("tensor", ["vector", "scalar"],
                           lambda tn, h=h, kv=kv, qb=qb, qs=qs, kb=kb, qsl=qsl, tkt=tkt, par=par:
                           tn.matmul(ps[4 + par][:],
                                     k_fm[kb:kb + 64, kv // 2, tkt * 128:(tkt + 1) * 128],
                                     q_fm[qb:qb + 64, qs, qsl],
                                     start=True, stop=True))
                    diag = tkt >= 4 * tq4
                    if diag:
                        off = 384 - (tkt * 128 - tq4 * 512)
                        p.step("vector", ["tensor"],
                               lambda v, par=par, off=off: v.tensor_add(
                                   ps[4 + par][:], ps[4 + par][:], M0[:, off:off + 512]))
                    p.step("scalar", ["tensor", "vector"],
                           lambda s, par=par: s.activation(
                               probs[par][:], ps[4 + par][:], AF.Exp, bias=shift_col[:]))
                    p.step("tensor", ["scalar"],
                           lambda tn, kv=kv, tkt=tkt, par=par, first=(tkt == 0), last=(tkt == ntk - 1):
                           tn.matmul(ps[3][0:65, :],
                                     v_aug[:, tkt, kv * 65:(kv + 1) * 65],
                                     probs[par][:], start=first, stop=last))
                p.step("scalar", ["tensor"], lambda s: s.copy(y_f32[:], ps[3][0:65, :]))
                p.step("vector", ["scalar"],
                       lambda v: v.reciprocal(y_f32[64:65, :], y_f32[64:65, :]))
                p.step("tensor", ["vector"],
                       lambda tn: tn.matmul(ps[6][0:64, :], ones_all[64:65, 0:64],
                                            y_f32[64:65, :], start=True, stop=True))
                p.step("vector", ["tensor", ("sync", p.n("sync"))],
                       lambda v: v.tensor_mul(ybuf[:], y_f32[0:64, :], ps[6][0:64, :]))
                p.step("sync", ["vector"],
                       lambda s, h=h, qsl=qsl: s.dma_start(
                           o_y.ap()[h * 64:(h + 1) * 64, qsl], ybuf[:]),
                       inc=16)
        p.step("sync", ["sync"], lambda s: None, inc=0)

        if p_ext is None:
            block = e(nc.Block())
            p.emit(block)
    return nc


_ROPE_PERM = np.concatenate([np.arange(0, HD, 2), np.arange(1, HD, 2)])


def _prep_C_inmaps(inputs, resB2):
    wq = inputs["L1_attn_wq"]
    wk = inputs["L1_attn_wk"]
    wv = inputs["L1_attn_wv"]
    maps = []
    for c in range(8):
        half = c % 2
        rows_q = np.concatenate([h * HD + _ROPE_PERM for h in range(half * NH, half * NH + NH)])
        rows_k = np.concatenate([h * HD + _ROPE_PERM for h in range(half * NKV, half * NKV + NKV)])
        rows_v = np.arange(half * NKV * HD, (half + 1) * NKV * HD)
        wqT_h = np.ascontiguousarray(wq[rows_q].T)
        wkT_h = np.ascontiguousarray(wk[rows_k].T)
        wvT_h = np.ascontiguousarray(wv[rows_v].T)
        pair = c // 2
        maps.append(dict(x1a=resB2[2 * pair]["o_x1"], x1b=resB2[2 * pair + 1]["o_x1"],
                         wqT=wqT_h, wkT=wkT_h, wvT=wvT_h))
    return maps


def run_C(inputs, resB2):
    nc = build_C()
    res = run_bass_kernel_spmd(nc, _prep_C_inmaps(inputs, resB2), core_ids=list(range(8)))
    return res.results




def build_D(p_ext=None, IO=None, split_y=False):
    if p_ext is not None:
        nc = p_ext.nc
    else:
        nc = bass.Bass("TRN2", target_bir_lowering=False)

    def T_in(name, shape, dtype):
        return IO[name] if IO is not None else nc.dram_tensor(name, shape, dtype, kind="ExternalInput")

    def T_out(name, shape, dtype):
        return IO[name] if IO is not None else nc.dram_tensor(name, shape, dtype, kind="ExternalOutput")
    y_full = None if split_y else T_in("y_full", [D, TC], BF16)
    x1T = T_in("x1T", [D, TC], F32)
    woT = T_in("woT", [D, D], F32)
    W1T = T_in("W1T", [D, MLPD], F32)
    W2T = T_in("W2T", [MLPD, MLPD], F32)
    W3T = T_in("W3T", [MLPD, D], F32)
    b1r = T_in("b1r", [128, 32], F32)
    b2r = T_in("b2r", [128, 32], F32)
    b3r = T_in("b3r", [128, 16], F32)
    o_out = T_out("o_out", [D, TC], F32)

    with ExitStack() as ctx:
        e = ctx.enter_context
        xn1 = e(nc.sbuf_tensor(_nm("xn1"), [128, NCH, TC], BF16))
        s1_bf = e(nc.sbuf_tensor(_nm("s1_bf"), [128, NCH, TC], BF16))
        rm = e(nc.sbuf_tensor(_nm("rm"), [1, 512], F32))
        rv = e(nc.sbuf_tensor(_nm("rv"), [1, 512], F32))
        rn = e(nc.sbuf_tensor(_nm("rn"), [1, 512], F32))
        rstd_b = e(nc.sbuf_tensor(_nm("rstd_b"), [128, TC], F32))
        nmr_b = e(nc.sbuf_tensor(_nm("nmr_b"), [128, TC], F32))
        sq5 = e(nc.sbuf_tensor(_nm("sq5"), [128, 512], F32))
        xc_t = e(nc.sbuf_tensor(_nm("xc_t"), [128, TC], F32))
        b1s = e(nc.sbuf_tensor(_nm("b1s"), [128, 32], F32))
        b2s = e(nc.sbuf_tensor(_nm("b2s"), [128, 32], F32))
        b3s = e(nc.sbuf_tensor(_nm("b3s"), [128, 16], F32))
        ones_all = e(nc.sbuf_tensor(_nm("ones_all"), [128, 128], F32))
        eps_col = e(nc.sbuf_tensor(_nm("eps_col"), [128, 1], F32))
        g_t = e(nc.sbuf_tensor(_nm("g_t"), [128, 512], F32))
        pr_t = e(nc.sbuf_tensor(_nm("pr_t"), [128, 512], F32))
        if split_y:
            msk0h = T_in("msk0", [128, 1], F32)
            msk1h = T_in("msk1", [128, 1], F32)
            m0c = e(nc.sbuf_tensor(_nm("m0c"), [128, 1], F32))
            m1c = e(nc.sbuf_tensor(_nm("m1c"), [128, 1], F32))
            ytmp0 = e(nc.sbuf_tensor(_nm("ytmp0"), [128, TC], BF16))
            ytmp1 = e(nc.sbuf_tensor(_nm("ytmp1"), [128, TC], BF16))
        ps = [e(nc.psum_tensor(_nm(f"ps{i}"), [128, 512], F32)) for i in range(8)]

        p = p_ext if p_ext is not None else Prog(nc, ctx)
        if p_ext is not None:
            p.fence()

        def consts(g):
            g.memset(ones_all[:], 1.0)
            return g.memset(eps_col[:], LN_EPS)
        p.step("gpsimd", [], consts)
        for f in [
            lambda s: s.dma_start(b1s[:], b1r.ap()),
            lambda s: s.dma_start(b2s[:], b2r.ap()),
            lambda s: s.dma_start(b3s[:], b3r.ap()),
        ]:
            p.step("sync", [], f, inc=16)

        for c in range(NCH):
            p.step("sync", [("tensor", p.n("tensor")), ("scalar", p.n("scalar"))],
                   lambda s, c=c: s.dma_start(
                       xc_t[:], x1T.ap().rearrange("(c p) t -> p c t", p=128)[:, c, :]),
                   inc=16)
            for tt in range(2):
                sl = slice(tt * 512, (tt + 1) * 512)
                p.step("scalar", ["sync", "tensor"],
                       lambda s, sl=sl: s.activation(sq5[:], xc_t[:, sl], AF.Square))
                p.step("tensor", ["scalar", "sync"],
                       lambda tn, c=c, sl=sl, tt=tt: (
                           tn.matmul(ps[tt][0:1, :], ones_all[:, 0:1], xc_t[:, sl],
                                     start=(c == 0), stop=(c == NCH - 1),
                                     skip_group_check=True),
                           tn.matmul(ps[2 + tt][0:1, :], ones_all[:, 0:1], sq5[:],
                                     start=(c == 0), stop=(c == NCH - 1),
                                     skip_group_check=True))[-1])
        for tt in range(2):
            sl = slice(tt * 512, (tt + 1) * 512)
            def fin_v(v, tt=tt):
                v.tensor_scalar_mul(rm[:], ps[tt][0:1, :], 1.0 / D)
                return v.tensor_scalar_mul(rv[:], ps[2 + tt][0:1, :], 1.0 / D)
            p.step("vector", ["tensor"], fin_v)
            p.step("vector", [], lambda v: v.tensor_mul(rn[:], rm[:], rm[:]))
            p.step("vector", [], lambda v: v.tensor_sub(rv[:], rv[:], rn[:]))
            p.step("scalar", ["vector"], lambda s: s.activation(rv[:], rv[:], AF.Ln, bias=eps_col[0:1, :]))
            p.step("scalar", [], lambda s: s.activation(rv[:], rv[:], AF.Exp, scale=-0.5))
            p.step("vector", ["scalar"], lambda v: v.scalar_tensor_tensor(
                rn[:], rm[:], -1.0, rv[:], ALU.mult, ALU.mult))
            p.step("tensor", ["vector"],
                   lambda tn, tt=tt: (
                       tn.matmul(ps[tt][:], ones_all[0:1, :], rv[:], start=True, stop=True),
                       tn.matmul(ps[2 + tt][:], ones_all[0:1, :], rn[:], start=True, stop=True))[-1])
            p.step("scalar", ["tensor"],
                   lambda s, tt=tt, sl=sl: (
                       s.copy(rstd_b[:, sl], ps[tt][:]),
                       s.copy(nmr_b[:, sl], ps[2 + tt][:]))[-1])
        for c in range(NCH):
            p.step("sync", [("vector", p.n("vector"))],
                   lambda s, c=c: s.dma_start(
                       xc_t[:], x1T.ap().rearrange("(c p) t -> p c t", p=128)[:, c, :]),
                   inc=16)
            for tt in range(2):
                sl = slice(tt * 512, (tt + 1) * 512)
                p.step("vector", ["sync", "scalar"],
                       lambda v, sl=sl: v.tensor_mul(ps[5][:], xc_t[:, sl], rstd_b[:, sl]))
                p.step("vector", [],
                       lambda v, c=c, sl=sl: v.tensor_add(xn1[:, c, sl], ps[5][:], nmr_b[:, sl]))

        with nc.sbuf_tensor(_nm("y_sb"), [128, NCH, TC], BF16) as y_sb, \
             nc.sbuf_tensor(_nm("woT_sb"), [128, NCH, D], BF16) as woT_sb:
            if split_y:
                yf = IO["yf"]
                p.step("sync", [], lambda s: s.dma_start(m0c[:], msk0h.ap()), inc=16)
                p.step("sync", [], lambda s: s.dma_start(m1c[:], msk1h.ap()), inc=16)
                for c in range(NCH):
                    p.step("sync", [("vector", p.n("vector"))],
                           lambda s, c=c: s.dma_start(
                               ytmp0[:], yf.ap().rearrange("(c p) t -> p c t", p=128)[:, c, 0:TC]),
                           inc=16)
                    p.step("sync", [],
                           lambda s, c=c: s.dma_start(
                               ytmp1[:], yf.ap().rearrange("(c p) t -> p c t", p=128)[:, c, TC:2 * TC]),
                           inc=16)
                    p.step("vector", ["sync"],
                           lambda v: v.tensor_scalar(ytmp1[:], ytmp1[:], m1c[:], None, ALU.mult))
                    p.step("vector", [],
                           lambda v, c=c: v.scalar_tensor_tensor(
                               y_sb[:, c, :], ytmp0[:], m0c[:], ytmp1[:], ALU.mult, ALU.add))
            else:
                p.step("sync", [], lambda s: s.dma_start(
                    y_sb[:], y_full.ap().rearrange("(c p) t -> p c t", p=128)), inc=16)
            p.step("gpsimd", [], lambda g: g.dma_start(
                woT_sb[:], woT.ap().rearrange("(c p) d -> p c d", p=128)), inc=16)
            for dc in range(NCH):
                for tt in range(2):
                    par = (dc * 2 + tt) % 2
                    sl = slice(tt * 512, (tt + 1) * 512)
                    def wo_mm(tn, dc=dc, sl=sl, par=par):
                        for c in range(NCH):
                            mm = tn.matmul(ps[6 + par][:], woT_sb[:, c, dc * 128:(dc + 1) * 128],
                                           y_sb[:, c, sl], start=(c == 0), stop=(c == NCH - 1))
                        return mm
                    p.step("tensor", ["sync", "gpsimd", ("scalar", p.n("scalar"))], wo_mm)
                    p.step("scalar", ["tensor"],
                           lambda s, dc=dc, sl=sl, par=par: s.copy(
                               s1_bf[:, dc, sl], ps[6 + par][:]))

        with nc.sbuf_tensor(_nm("h1"), [128, 32, 512], BF16) as h1, \
             nc.sbuf_tensor(_nm("h2"), [128, 32, 512], BF16) as h2, \
             nc.sbuf_tensor(_nm("w1c0"), [128, NCH, 128], BF16) as w1c0, \
             nc.sbuf_tensor(_nm("w1c1"), [128, NCH, 128], BF16) as w1c1, \
             nc.sbuf_tensor(_nm("w2c0"), [128, 32, 128], BF16) as w2c0, \
             nc.sbuf_tensor(_nm("w2c1"), [128, 32, 128], BF16) as w2c1:
            w1c = [w1c0, w1c1]
            w2c = [w2c0, w2c1]

            def stream(ntask, dma_fn, mm_fn):
                after = {}
                reader = {}
                def dma(k):
                    waits = [("tensor", reader[k - 2])] if k >= 2 else \
                            [("tensor", p.n("tensor"))]
                    dma_fn(k, k % 2, waits)
                    after[k] = p.n("gpsimd")
                dma(0)
                if ntask > 1:
                    dma(1)
                for k in range(ntask):
                    mm_fn(k, k % 2, after[k])
                    reader[k] = p.n("tensor")
                    if k + 2 < ntask:
                        dma(k + 2)

            for tt in range(2):
                sl = slice(tt * 512, (tt + 1) * 512)
                def d1(k, par, waits):
                    p.step("gpsimd", waits,
                           lambda g, hc=k, par=par: g.dma_start(
                               w1c[par][:], W1T.ap()[:, hc * 128:(hc + 1) * 128]
                               .rearrange("(c p) f -> p c f", p=128)),
                           inc=16)
                def m1(k, par, aft, sl=sl):
                    def h1_mm(tn, par=par, sl=sl):
                        for c in range(NCH):
                            mm = tn.matmul(ps[par][:], w1c[par][:, c, :], xn1[:, c, sl],
                                           start=(c == 0), stop=(c == NCH - 1))
                        return mm
                    p.step("tensor", [("gpsimd", aft), ("scalar", p.n("scalar")),
                                      ("vector", p.n("vector"))], h1_mm)
                    p.step("scalar", ["tensor"],
                           lambda s, hc=k, par=par: s.activation(
                               h1[:, hc, :], ps[par][:], AF.Relu, bias=b1s[:, hc:hc + 1]))
                stream(32, d1, m1)
                def d2(k, par, waits):
                    p.step("gpsimd", waits,
                           lambda g, hc=k, par=par: g.dma_start(
                               w2c[par][:], W2T.ap()[:, hc * 128:(hc + 1) * 128]
                               .rearrange("(c p) f -> p c f", p=128)),
                           inc=16)
                def m2(k, par, aft):
                    def h2_mm(tn, par=par):
                        for q in range(32):
                            mm = tn.matmul(ps[2 + par][:], w2c[par][:, q, :], h1[:, q, :],
                                           start=(q == 0), stop=(q == 31))
                        return mm
                    p.step("tensor", [("gpsimd", aft), ("scalar", p.n("scalar"))], h2_mm)
                    p.step("scalar", ["tensor"],
                           lambda s, hc=k, par=par: s.activation(
                               h2[:, hc, :], ps[2 + par][:], AF.Relu, bias=b2s[:, hc:hc + 1]))
                stream(32, d2, m2)
                def d3(k, par, waits):
                    p.step("gpsimd", waits,
                           lambda g, dc=k, par=par: g.dma_start(
                               w2c[par][:], W3T.ap()[:, dc * 128:(dc + 1) * 128]
                               .rearrange("(c p) f -> p c f", p=128)),
                           inc=16)
                def m3(k, par, aft, sl=sl):
                    def w3_mm(tn, par=par):
                        for q in range(32):
                            mm = tn.matmul(ps[4 + par][:], w2c[par][:, q, :], h2[:, q, :],
                                           start=(q == 0), stop=(q == 31))
                        return mm
                    p.step("tensor", [("gpsimd", aft), ("scalar", p.n("scalar")),
                                      ("vector", p.n("vector")), ("sync", p.n("sync"))], w3_mm)
                    p.step("scalar", ["tensor"],
                           lambda s, dc=k, par=par: s.activation(
                               g_t[:], ps[4 + par][:], AF.Silu, bias=b3s[:, dc:dc + 1]))
                    p.step("vector", ["scalar", ("sync", p.n("sync"))],
                           lambda v, dc=k, sl=sl: v.tensor_mul(
                               pr_t[:], g_t[:], s1_bf[:, dc, sl]))
                    p.step("sync", [("vector", p.n("vector"))],
                           lambda s, dc=k, sl=sl: s.dma_start(
                               xc_t[:, 0:512],
                               x1T.ap().rearrange("(c p) t -> p c t", p=128)[:, dc, sl]),
                           inc=16)
                    p.step("vector", ["sync"],
                           lambda v: v.tensor_add(pr_t[:], pr_t[:], xc_t[:, 0:512]))
                    p.step("sync", ["vector"],
                           lambda s, dc=k, sl=sl: s.dma_start(
                               o_out.ap().rearrange("(c p) t -> p c t", p=128)[:, dc, sl],
                               pr_t[:]),
                           inc=16)
                stream(NCH, d3, m3)
        p.step("sync", ["sync"], lambda s: None, inc=0)

        if p_ext is None:
            block = e(nc.Block())
            p.emit(block)
    return nc


def _prep_D_inmaps(inputs, resB2, resC):
    woT = np.ascontiguousarray(inputs["L1_attn_wo"].T)
    W1T = np.ascontiguousarray(inputs["L1_ffn_W1"].T)
    W2T = np.ascontiguousarray(inputs["L1_ffn_W2"].T)
    W3T = np.ascontiguousarray(inputs["L1_ffn_W3"].T)
    b1r = np.ascontiguousarray(inputs["L1_ffn_b1"].reshape(32, 128).T)
    b2r = np.ascontiguousarray(inputs["L1_ffn_b2"].reshape(32, 128).T)
    b3r = np.ascontiguousarray(inputs["L1_ffn_b3"].reshape(16, 128).T)
    maps = []
    for c in range(8):
        b, h = c // 2, c % 2
        t0 = h * TC
        y_full = np.ascontiguousarray(np.concatenate(
            [resC[2 * b]["o_y"][:, t0:t0 + TC], resC[2 * b + 1]["o_y"][:, t0:t0 + TC]], 0))
        maps.append(dict(y_full=y_full, x1T=resB2[c]["o_x1"], woT=woT,
                         W1T=W1T, W2T=W2T, W3T=W3T, b1r=b1r, b2r=b2r, b3r=b3r))
    return maps


def run_D(inputs, resB2, resC):
    nc = build_D()
    res = run_bass_kernel_spmd(nc, _prep_D_inmaps(inputs, resB2, resC), core_ids=list(range(8)))
    return res.results




PAIRS = [[0, 1], [2, 3], [4, 5], [6, 7]]
_FUSED_DEBUG = False


class _Rows:

    def __init__(self, h, r0, r1):
        self.h, self.r0, self.r1 = h, r0, r1

    def ap(self):
        return self.h.ap()[self.r0:self.r1, :]


def build_fused():
    nc = bass.Bass("TRN2", target_bir_lowering=False)
    EI = lambda n, sh, dt: nc.dram_tensor(n, sh, dt, kind="ExternalInput")
    ins = {}
    for n, sh, dt in [
        ("xT", [D, TC], F32), ("xhT", [D, 3], F32), ("conv_w4", [D, 4], F32),
        ("conv_b", [128, NCH], F32), ("xprojT", [D, 160], F32),
        ("dtprojT", [DTR, D], F32), ("dtproj_b", [128, NCH], F32),
        ("A_log", [D, NST], F32), ("D_par", [128, NCH], F32),
        ("w1_all", [NEXP, D, DFF], F32), ("w2_all", [NEXP, DFF, D], F32),
        ("b1_all", [128, NEXP, NFC], F32), ("sh_b2", [128, 2, NCH], F32),
        ("rt_b2", [16, D], F32), ("cenT", [D, 16], F32), ("cen_bias", [1, 16], F32),
        ("wqT", [D, NH * HD], F32), ("wkT", [D, NKV * HD], F32), ("wvT", [D, NKV * HD], F32),
        ("woT", [D, D], F32), ("W1T", [D, MLPD], F32), ("W2T", [MLPD, MLPD], F32),
        ("W3T", [MLPD, D], F32), ("b1r", [128, 32], F32), ("b2r", [128, 32], F32),
        ("b3r", [128, 16], F32), ("msk", [128, 1], F32), ("msk0", [128, 1], F32),
        ("msk1", [128, 1], F32),
    ]:
        ins[n] = EI(n, sh, dt)
    o_out = nc.dram_tensor("o_out", [D, TC], F32, kind="ExternalOutput")
    dbg = {}
    if _FUSED_DEBUG:
        dbg = {n: nc.dram_tensor("dbg_" + n, sh, dt, kind="ExternalOutput")
               for n, sh, dt in [("xn", [D, TC], BF16), ("dl", [D, TC], F32),
                                  ("s", [D, TC], F32), ("x1", [D, TC], F32),
                                  ("y", [NH * HD, T], BF16), ("hf2", [256, NCH * NST], F32)]}
    d_xn = nc.dram_tensor("d_xn", [D, TC], BF16)
    d_xc = nc.dram_tensor("d_xc", [D, TC], BF16)
    d_dl = nc.dram_tensor("d_dl", [D, TC], F32)
    d_bc = nc.dram_tensor("d_bc", [32, TC], F32)
    d_hf = nc.dram_tensor("d_hf", [128, NCH * NST], F32)
    d_hf2 = nc.dram_tensor("d_hf2", [256, NCH * NST], F32)
    d_s = nc.dram_tensor("d_s", [D, TC], F32)
    d_x1 = nc.dram_tensor("d_x1", [D, TC], F32)
    d_x1f = nc.dram_tensor("d_x1f", [2 * D, TC], F32)
    d_y = nc.dram_tensor("d_y", [NH * HD, T], BF16)
    d_yf = nc.dram_tensor("d_yf", [2 * NH * HD, T], BF16)

    with ExitStack() as octx:
        p = Prog(nc, octx)
        IO_A = dict(xT=ins["xT"], xhT=ins["xhT"], conv_w4=ins["conv_w4"],
                    conv_b=ins["conv_b"], xprojT=ins["xprojT"], dtprojT=ins["dtprojT"],
                    dtproj_b=ins["dtproj_b"], A_log=ins["A_log"],
                    o_xn=d_xn, o_xc=d_xc, o_dl=d_dl, o_bc=d_bc, o_hf=d_hf)
        build_A(p_ext=p, IO=IO_A)
        p.step("gpsimd", ["sync"], lambda g: g.collective_compute(
            "AllGather", ALU.bypass, replica_groups=PAIRS,
            ins=[d_hf.ap().opt()], outs=[d_hf2.ap().opt()]))
        IO_B1 = dict(i_dl=d_dl, i_xc=d_xc, i_bc=d_bc, A_log=ins["A_log"],
                     D_par=ins["D_par"], i_xn=d_xn, i_hi=_Rows(d_hf2, 0, 128),
                     msk=ins["msk"], o_s=d_s)
        build_B1(p_ext=p, IO=IO_B1, masked=True)
        IO_B2 = dict(i_xn=d_xn, i_s=d_s, xT=ins["xT"], w1_all=ins["w1_all"],
                     w2_all=ins["w2_all"], b1_all=ins["b1_all"], sh_b2=ins["sh_b2"],
                     rt_b2=ins["rt_b2"], cenT=ins["cenT"], cen_bias=ins["cen_bias"],
                     o_x1=d_x1)
        build_B2(p_ext=p, IO=IO_B2)
        p.step("gpsimd", ["sync"], lambda g: g.collective_compute(
            "AllGather", ALU.bypass, replica_groups=PAIRS,
            ins=[d_x1.ap().opt()], outs=[d_x1f.ap().opt()]))
        IO_C = dict(x1a=_Rows(d_x1f, 0, D), x1b=_Rows(d_x1f, D, 2 * D),
                    wqT=ins["wqT"], wkT=ins["wkT"], wvT=ins["wvT"], o_y=d_y)
        build_C(p_ext=p, IO=IO_C)
        p.step("gpsimd", ["sync"], lambda g: g.collective_compute(
            "AllGather", ALU.bypass, replica_groups=PAIRS,
            ins=[d_y.ap().opt()], outs=[d_yf.ap().opt()]))
        IO_D = dict(yf=d_yf, x1T=d_x1, woT=ins["woT"], W1T=ins["W1T"],
                    W2T=ins["W2T"], W3T=ins["W3T"], b1r=ins["b1r"], b2r=ins["b2r"],
                    b3r=ins["b3r"], msk0=ins["msk0"], msk1=ins["msk1"], o_out=o_out)
        build_D(p_ext=p, IO=IO_D, split_y=True)
        if _FUSED_DEBUG:
            p.fence()
            for nm, src in [("xn", d_xn), ("dl", d_dl), ("s", d_s), ("x1", d_x1),
                            ("y", d_y), ("hf2", d_hf2)]:
                p.step("sync", ["sync", "gpsimd", "vector", "scalar", "tensor"],
                       lambda s, nm=nm, src=src: s.dma_start(dbg[nm].ap(), src.ap()), inc=16)
            p.step("sync", ["sync"], lambda s: None, inc=0)
        block = octx.enter_context(nc.Block())
        p.emit(block)
    return nc


def _prep_fused_inmaps(inputs):
    mA = _prep_A_inmaps(inputs)
    w1_all = np.ascontiguousarray(np.concatenate(
        [inputs["L0_ffn_shared_w1"], inputs["L0_ffn_routed_w1"]], 0))
    w2_all = np.ascontiguousarray(np.concatenate(
        [inputs["L0_ffn_shared_w2"], inputs["L0_ffn_routed_w2"]], 0))
    b1_all = np.ascontiguousarray(np.concatenate(
        [inputs["L0_ffn_shared_b1"], inputs["L0_ffn_routed_b1"]], 0)
        .reshape(NEXP, NFC, 128).transpose(2, 0, 1))
    sh_b2 = np.ascontiguousarray(
        inputs["L0_ffn_shared_b2"].reshape(2, NCH, 128).transpose(2, 0, 1))
    rt_b2 = np.ascontiguousarray(inputs["L0_ffn_routed_b2"])
    cenT = np.ascontiguousarray(inputs["L0_ffn_centroids"].T)
    cen_bias = np.ascontiguousarray(inputs["L0_ffn_bias"].reshape(1, 16))
    A_log = np.ascontiguousarray(inputs["L0_attn_A_log"])
    D_par = np.ascontiguousarray(inputs["L0_attn_D_param"].reshape(NCH, 128).T)
    woT = np.ascontiguousarray(inputs["L1_attn_wo"].T)
    W1T = np.ascontiguousarray(inputs["L1_ffn_W1"].T)
    W2T = np.ascontiguousarray(inputs["L1_ffn_W2"].T)
    W3T = np.ascontiguousarray(inputs["L1_ffn_W3"].T)
    b1r = np.ascontiguousarray(inputs["L1_ffn_b1"].reshape(32, 128).T)
    b2r = np.ascontiguousarray(inputs["L1_ffn_b2"].reshape(32, 128).T)
    b3r = np.ascontiguousarray(inputs["L1_ffn_b3"].reshape(16, 128).T)
    wq = inputs["L1_attn_wq"]
    wk = inputs["L1_attn_wk"]
    wv = inputs["L1_attn_wv"]
    maps = []
    for c in range(8):
        half = c % 2
        rows_q = np.concatenate([h * HD + _ROPE_PERM for h in range(half * NH, half * NH + NH)])
        rows_k = np.concatenate([h * HD + _ROPE_PERM for h in range(half * NKV, half * NKV + NKV)])
        rows_v = np.arange(half * NKV * HD, (half + 1) * NKV * HD)
        m = dict(mA[c])
        m.update(D_par=D_par, w1_all=w1_all, w2_all=w2_all, b1_all=b1_all,
                 sh_b2=sh_b2, rt_b2=rt_b2, cenT=cenT, cen_bias=cen_bias,
                 wqT=np.ascontiguousarray(wq[rows_q].T),
                 wkT=np.ascontiguousarray(wk[rows_k].T),
                 wvT=np.ascontiguousarray(wv[rows_v].T),
                 woT=woT, W1T=W1T, W2T=W2T, W3T=W3T, b1r=b1r, b2r=b2r, b3r=b3r,
                 msk=np.full((128, 1), float(half), np.float32),
                 msk0=np.full((128, 1), 1.0 - float(half), np.float32),
                 msk1=np.full((128, 1), float(half), np.float32))
        maps.append(m)
    return maps


def kernel_fused(inputs):
    nc = build_fused()
    res = run_bass_kernel_spmd(nc, _prep_fused_inmaps(inputs), core_ids=list(range(8)))
    out = np.empty((4, T, D), np.float32)
    for c in range(8):
        b, h = c // 2, c % 2
        out[b, h * TC:(h + 1) * TC, :] = res.results[c]["o_out"].T
    return out



def _fm(a):
    return np.ascontiguousarray(a.T)


def _prep_A_inmaps(inputs):
    x = inputs["L0_attn_x"]
    conv_w4 = np.ascontiguousarray(inputs["L0_attn_conv_w"].reshape(D, 4))
    conv_b = np.ascontiguousarray(inputs["L0_attn_conv_b"].reshape(NCH, 128).T)
    xprojT = np.ascontiguousarray(inputs["L0_attn_xproj_w"].T)
    dtprojT = np.ascontiguousarray(inputs["L0_attn_dtproj_w"].T)
    dtproj_b = np.ascontiguousarray(inputs["L0_attn_dtproj_b"].reshape(NCH, 128).T)
    A_log = np.ascontiguousarray(inputs["L0_attn_A_log"])
    maps = []
    for c in range(8):
        b, h = c // 2, c % 2
        t0 = h * TC
        xT = _fm(x[b, t0:t0 + TC])
        if h == 0:
            xhT = np.zeros((D, 3), np.float32)
        else:
            xhT = _fm(x[b, t0 - 3:t0])
        maps.append(dict(xT=xT, xhT=xhT, conv_w4=conv_w4, conv_b=conv_b,
                         xprojT=xprojT, dtprojT=dtprojT, dtproj_b=dtproj_b,
                         A_log=A_log))
    return maps


def run_A(inputs, stages=99):
    nc = build_A(stages)
    res = run_bass_kernel_spmd(nc, _prep_A_inmaps(inputs), core_ids=list(range(8)))
    return res.results


def kernel(**inputs):
    inputs = {k: np.asarray(v) for k, v in inputs.items()}
    return kernel_fused(inputs)


def kernel_unfused(**inputs):
    inputs = {k: np.asarray(v) for k, v in inputs.items()}
    resA = run_A(inputs)
    resB1 = run_B1(inputs, resA)
    resB2 = run_B2(inputs, resA, resB1)
    resC = run_C(inputs, resB2)
    resD = run_D(inputs, resB2, resC)
    out = np.empty((4, T, D), np.float32)
    for c in range(8):
        b, h = c // 2, c % 2
        out[b, h * TC:(h + 1) * TC, :] = resD[c]["o_out"].T
    return out


# version 06 (final)
# speedup vs baseline: 1.0457x; 1.0457x over previous
import numpy as np
import ml_dtypes
from contextlib import ExitStack

import concourse.bass as bass
import concourse.mybir as mybir
from concourse.bass_utils import run_bass_kernel_spmd

F32 = mybir.dt.float32
BF16 = mybir.dt.bfloat16
I32 = mybir.dt.int32
AF = mybir.ActivationFunctionType
ALU = mybir.AluOpType
AX = mybir.AxisListType

D = 2048
NCH = 16
TC = 1024
T = 2048
NST = 16
DTR = 128
DFF = 1408
NFC = 11
NEXP = 18
NH = 16
NKV = 4
HD = 64
MLPD = 4096
LN_EPS = 1e-5
EXP_SHIFT = -10.0

BF = ml_dtypes.bfloat16
ENGS = ("sync", "gpsimd", "tensor", "vector", "scalar")

_NM_CTR = [0]


def _nm(name):
    _NM_CTR[0] += 1
    return f"{name}_{_NM_CTR[0]}"



class Prog:

    def __init__(self, nc, ctx):
        self.nc = nc
        self.ctx = ctx
        self.steps = {e: [] for e in ENGS}
        self.sem = {e: self._mk(nc, ctx, "s_" + e) for e in ENGS}
        self._fence_pending = {}

    def fence(self):
        snap = {e: self.sem[e].n for e in ENGS}
        self._fence_pending = {e: snap for e in ENGS}

    @staticmethod
    def _mk(nc, ctx, name):
        class _S:
            pass
        s = _S()
        s.h = ctx.enter_context(nc.semaphore(name))
        s.n = 0
        return s

    def step(self, eng, waits, f, inc=1):
        snap = []
        own = self.sem[eng]
        if eng != "tensor" and inc < 16 and own.n > 0:
            snap.append((own, own.n))
        if eng in self._fence_pending:
            fsnap = self._fence_pending.pop(eng)
            for o in ENGS:
                if o != eng and fsnap[o] > 0:
                    snap.append((self.sem[o], fsnap[o]))
        for w in waits:
            if isinstance(w, tuple):
                s, n = self.sem[w[0]], w[1]
            else:
                s, n = self.sem[w], self.sem[w].n
            if n > 0 and not (s is own and snap and snap[0][0] is own):
                snap.append((s, n))
        self.steps[eng].append((snap, f, own, inc))
        own.n += inc
        return own.n

    def n(self, eng):
        return self.sem[eng].n

    def emit(self, block):
        for eng in ENGS:
            if not self.steps[eng]:
                continue
            steps = self.steps[eng]

            def body(e, steps=steps):
                for snap, f, own, inc in steps:
                    for s, cnt in snap:
                        e.wait_ge(s.h, cnt)
                    last = f(e)
                    if last is not None and inc > 0:
                        last.then_inc(own.h, inc)
            getattr(block, eng)(body)


def _ln_featmajor(p, src, out_bf, t_len, ones_f, ones_rf, rows, bcast, ps, sq_t,
                  eps_col, extra_wait=()):
    ntt = (t_len + 511) // 512
    for tt in range(ntt):
        w = min(512, t_len - tt * 512)
        sl = slice(tt * 512, tt * 512 + w)
        for c in range(NCH):
            p.step("scalar", ["tensor"] + list(extra_wait),
                   lambda s, c=c, sl=sl: s.activation(sq_t[:, sl], src(c)[:, sl], AF.Square))
            p.step("tensor", ["scalar"] + list(extra_wait),
                   lambda tn, c=c, sl=sl, tt=tt, w=w: (
                       tn.matmul(ps[0][0:1, 0:w], ones_f[:], src(c)[:, sl],
                                 start=(c == 0), stop=(c == NCH - 1),
                                 skip_group_check=True),
                       tn.matmul(ps[1][0:1, 0:w], ones_f[:], sq_t[:, sl],
                                 start=(c == 0), stop=(c == NCH - 1),
                                 skip_group_check=True))[-1])
        def fin_v(v, sl=sl, w=w):
            v.tensor_scalar_mul(rows["m"][:, sl], ps[0][0:1, 0:w], 1.0 / D)
            return v.tensor_scalar_mul(rows["var"][:, sl], ps[1][0:1, 0:w], 1.0 / D)
        p.step("vector", ["tensor"], fin_v)
        p.step("vector", [],
               lambda v, sl=sl: v.tensor_mul(rows["mm"][:, sl], rows["m"][:, sl], rows["m"][:, sl]))
        p.step("vector", [],
               lambda v, sl=sl: v.tensor_sub(rows["var"][:, sl], rows["var"][:, sl], rows["mm"][:, sl]))
        p.step("scalar", ["vector"],
               lambda s, sl=sl: s.activation(rows["rstd"][:, sl], rows["var"][:, sl], AF.Ln, bias=eps_col[0:1, :]))
        p.step("scalar", ["scalar"],
               lambda s, sl=sl: s.activation(rows["rstd"][:, sl], rows["rstd"][:, sl], AF.Exp, scale=-0.5))
        p.step("vector", ["scalar"],
               lambda v, sl=sl: v.scalar_tensor_tensor(
                   rows["nmr"][:, sl], rows["m"][:, sl], -1.0, rows["rstd"][:, sl],
                   ALU.mult, ALU.mult))
        p.step("tensor", ["vector"],
               lambda tn, sl=sl, w=w: (
                   tn.matmul(ps[0][:, 0:w], ones_rf[:], rows["rstd"][:, sl],
                             start=True, stop=True),
                   tn.matmul(ps[1][:, 0:w], ones_rf[:], rows["nmr"][:, sl],
                             start=True, stop=True))[-1])
        p.step("scalar", ["tensor"],
               lambda s, sl=sl, w=w: (
                   s.copy(bcast[0][:, sl], ps[0][:, 0:w]),
                   s.copy(bcast[1][:, sl], ps[1][:, 0:w]))[-1])
    for c in range(NCH):
        p.step("vector", ["scalar"],
               lambda v, c=c: v.tensor_mul(sq_t[:], src(c), bcast[0][:, 0:t_len]))
        p.step("vector", [],
               lambda v, c=c: v.tensor_add(out_bf(c), sq_t[:], bcast[1][:, 0:t_len]))


def _softplus(p, z_src, out, tmp1, tmp2):
    p.step("scalar", ["vector"], lambda s: s.activation(tmp1[:], z_src, AF.Abs))
    p.step("scalar", ["scalar"], lambda s: s.activation(tmp1[:], tmp1[:], AF.Exp, scale=-1.0))
    p.step("scalar", ["scalar"], lambda s: s.activation(tmp1[:], tmp1[:], AF.Ln, bias=1.0))
    p.step("vector", ["scalar"], lambda v: v.tensor_scalar_max(tmp2[:], z_src, 0.0))
    p.step("vector", ["vector"], lambda v: v.tensor_add(out[:], tmp1[:], tmp2[:]))



def build_A(stages=99, p_ext=None, IO=None):
    if p_ext is not None:
        nc = p_ext.nc
    else:
        nc = bass.Bass("TRN2", target_bir_lowering=False)

    def T_in(name, shape, dtype):
        return IO[name] if IO is not None else nc.dram_tensor(name, shape, dtype, kind="ExternalInput")

    def T_out(name, shape, dtype):
        return IO[name] if IO is not None else nc.dram_tensor(name, shape, dtype, kind="ExternalOutput")
    xT = T_in("xT", [D, TC], F32)
    xhT = T_in("xhT", [D, 3], F32)
    conv_w4 = T_in("conv_w4", [D, 4], F32)
    conv_b = T_in("conv_b", [128, NCH], F32)
    xprojT = T_in("xprojT", [D, 160], F32)
    dtprojT = T_in("dtprojT", [DTR, D], F32)
    dtproj_b = T_in("dtproj_b", [128, NCH], F32)
    A_log = T_in("A_log", [D, NST], F32)

    o_xn = T_out("o_xn", [D, TC], BF16)
    o_xc = T_out("o_xc", [D, TC], BF16)
    o_dl = T_out("o_dl", [D, TC], F32)
    o_bc = T_out("o_bc", [32, TC], F32)
    o_hf = T_out("o_hf", [128, NCH * NST], F32)

    with ExitStack() as ctx:
        e = ctx.enter_context
        buf0 = e(nc.sbuf_tensor(_nm("buf0"), [128, NCH, TC], F32))
        xn = e(nc.sbuf_tensor(_nm("xn"), [128, NCH, TC], BF16))
        xcb = e(nc.sbuf_tensor(_nm("xcb"), [128, NCH, TC], BF16))
        tA = e(nc.sbuf_tensor(_nm("tA"), [128, TC], F32))
        tB = e(nc.sbuf_tensor(_nm("tB"), [128, TC], F32))
        tC = e(nc.sbuf_tensor(_nm("tC"), [128, TC], F32))
        tD = e(nc.sbuf_tensor(_nm("tD"), [128, TC], F32))
        tE = e(nc.sbuf_tensor(_nm("tE"), [128, TC], F32))
        a_t = [e(nc.sbuf_tensor(_nm(f"a_t{i}"), [128, TC], F32)) for i in range(2)]
        xh = e(nc.sbuf_tensor(_nm("xh"), [128, NCH, 3], F32))
        xh_bf = e(nc.sbuf_tensor(_nm("xh_bf"), [128, NCH, 3], BF16))
        cw = e(nc.sbuf_tensor(_nm("cw"), [128, NCH, 4], F32))
        cb = e(nc.sbuf_tensor(_nm("cb"), [128, NCH], F32))
        dpb = e(nc.sbuf_tensor(_nm("dpb"), [128, NCH], F32))
        Asb = e(nc.sbuf_tensor(_nm("Asb"), [128, NCH, NST], F32))
        xprojT_bf = e(nc.sbuf_tensor(_nm("xprojT_bf"), [128, NCH, 160], BF16))
        dtprojT_bf = e(nc.sbuf_tensor(_nm("dtprojT_bf"), [128, D], BF16))
        ones_bf = e(nc.sbuf_tensor(_nm("ones_bf"), [1, 128], BF16))
        ones_f = e(nc.sbuf_tensor(_nm("ones_f"), [128, 1], F32))
        ones_rf = e(nc.sbuf_tensor(_nm("ones_rf"), [1, 128], F32))
        eps_col = e(nc.sbuf_tensor(_nm("eps_col"), [128, 1], F32))
        rm = e(nc.sbuf_tensor(_nm("rm"), [1, TC], F32))
        rv = e(nc.sbuf_tensor(_nm("rv"), [1, TC], F32))
        rn = e(nc.sbuf_tensor(_nm("rn"), [1, TC], F32))
        rows = {"m": rm, "var": rv, "mm": rn, "rstd": rv, "nmr": rn}
        rstd_b = e(nc.sbuf_tensor(_nm("rstd_b"), [128, TC], BF16))
        nmr_b = e(nc.sbuf_tensor(_nm("nmr_b"), [128, TC], BF16))
        xe = e(nc.sbuf_tensor(_nm("xe"), [128, TC + 3], BF16))
        dt_bf = e(nc.sbuf_tensor(_nm("dt_bf"), [128, TC], BF16))
        bc_f = e(nc.sbuf_tensor(_nm("bc_f"), [32, TC], F32))
        bc_bf = e(nc.sbuf_tensor(_nm("bc_bf"), [32, TC], BF16))
        selB = e(nc.sbuf_tensor(_nm("selB"), [32, NST, 128], BF16))
        selC = e(nc.sbuf_tensor(_nm("selC"), [32, NST, 128], BF16))
        hfin = e(nc.sbuf_tensor(_nm("hfin"), [128, NCH * NST], F32))
        ps = [e(nc.psum_tensor(_nm(f"ps{i}"), [128, 512], F32)) for i in range(8)]

        p = p_ext if p_ext is not None else Prog(nc, ctx)
        if p_ext is not None:
            p.fence()

        for f in [
            lambda s: s.dma_start(buf0[:], xT.ap().rearrange("(c p) t -> p c t", p=128)),
            lambda s: s.dma_start(xh[:], xhT.ap().rearrange("(c p) t -> p c t", p=128)),
            lambda s: s.dma_start(cw[:], conv_w4.ap().rearrange("(c p) k -> p c k", p=128)),
            lambda s: s.dma_start(cb[:], conv_b.ap()),
            lambda s: s.dma_start(dpb[:], dtproj_b.ap()),
            lambda s: s.dma_start(Asb[:], A_log.ap().rearrange("(c p) n -> p c n", p=128)),
        ]:
            p.step("sync", [], f, inc=16)
        p.step("gpsimd", [], lambda g: g.dma_start(
            xprojT_bf[:], xprojT.ap().rearrange("(c p) f -> p c f", p=128)), inc=16)
        p.step("gpsimd", [], lambda g: g.dma_start(dtprojT_bf[:], dtprojT.ap()), inc=16)

        def consts(g):
            g.memset(ones_bf[:], 1.0)
            g.memset(ones_rf[:], 1.0)
            g.memset(ones_f[:], 1.0)
            g.memset(eps_col[:], LN_EPS)
            g.memset(selB[:], 0.0)
            return g.memset(selC[:], 0.0)
        p.step("gpsimd", [], consts)
        p.step("gpsimd", [], lambda g: g.affine_select(
            out=selB[:], in_=selB[:], compare_op=ALU.not_equal, fill=1.0, base=0,
            pattern=[[-1, NST], [0, 128]], channel_multiplier=1))
        p.step("gpsimd", [], lambda g: g.affine_select(
            out=selC[:], in_=selC[:], compare_op=ALU.not_equal, fill=1.0, base=-16,
            pattern=[[-1, NST], [0, 128]], channel_multiplier=1))

        p.step("scalar", ["sync"], lambda s: s.activation(Asb[:], Asb[:], AF.Exp))
        p.step("scalar", ["scalar"], lambda s: s.mul(Asb[:], Asb[:], -1.0))

        p.step("vector", ["sync", "gpsimd"], lambda v: None, inc=0)
        _ln_featmajor(p, lambda c: buf0[:, c, :], lambda c: xn[:, c, :], TC,
                      ones_f, ones_rf, rows, (rstd_b, nmr_b), (ps[0], ps[1]), tA,
                      eps_col, extra_wait=["sync", "gpsimd"])

        if stages >= 2:
            _build_A_halo(p, locals())
        if stages >= 3:
            _build_A_conv(p, locals())
        if stages >= 4:
            _build_A_scan(p, locals(), stages)
        _build_A_stores(p, locals(), stages)
        if p_ext is None:
            block = e(nc.Block())
            p.emit(block)
    return nc


def _build_A_halo(p, L):
    nc = L["nc"]; tA = L["tA"]; xh = L["xh"]; xh_bf = L["xh_bf"]
    ones_f = L["ones_f"]; ones_rf = L["ones_rf"]; ps = L["ps"]
    rm = L["rm"]; rv = L["rv"]; rn = L["rn"]
    rstd_b = L["rstd_b"]; nmr_b = L["nmr_b"]; eps_col = L["eps_col"]
    if True:
        for c in range(NCH):
            p.step("scalar", ["tensor", "vector"],
                   lambda s, c=c: s.activation(tA[:, c * 3:(c + 1) * 3], xh[:, c, :], AF.Square))
        for c in range(NCH):
            p.step("tensor", ["scalar"],
                   lambda tn, c=c: (
                       tn.matmul(ps[0][0:1, 0:3], ones_f[:], xh[:, c, :],
                                 start=(c == 0), stop=(c == NCH - 1),
                                 skip_group_check=True),
                       tn.matmul(ps[1][0:1, 0:3], ones_f[:], tA[:, c * 3:(c + 1) * 3],
                                 start=(c == 0), stop=(c == NCH - 1),
                                 skip_group_check=True))[-1])
        def fin_h(v):
            v.tensor_scalar_mul(rm[:, 0:3], ps[0][0:1, 0:3], 1.0 / D)
            return v.tensor_scalar_mul(rv[:, 0:3], ps[1][0:1, 0:3], 1.0 / D)
        p.step("vector", ["tensor"], fin_h)
        p.step("vector", [], lambda v: v.tensor_mul(rn[:, 0:3], rm[:, 0:3], rm[:, 0:3]))
        p.step("vector", [], lambda v: v.tensor_sub(rv[:, 0:3], rv[:, 0:3], rn[:, 0:3]))
        p.step("scalar", ["vector"], lambda s: s.activation(rv[:, 0:3], rv[:, 0:3], AF.Ln, bias=eps_col[0:1, :]))
        p.step("scalar", ["scalar"], lambda s: s.activation(rv[:, 0:3], rv[:, 0:3], AF.Exp, scale=-0.5))
        p.step("vector", ["scalar"],
               lambda v: v.scalar_tensor_tensor(rn[:, 0:3], rm[:, 0:3], -1.0, rv[:, 0:3],
                                                ALU.mult, ALU.mult))
        p.step("tensor", ["vector"],
               lambda tn: (
                   tn.matmul(ps[0][:, 0:3], ones_rf[:], rv[:, 0:3], start=True, stop=True),
                   tn.matmul(ps[1][:, 0:3], ones_rf[:], rn[:, 0:3], start=True, stop=True))[-1])
        p.step("scalar", ["tensor"],
               lambda s: (s.copy(rstd_b[:, 0:3], ps[0][:, 0:3]),
                          s.copy(nmr_b[:, 0:3], ps[1][:, 0:3]))[-1])
        for c in range(NCH):
            p.step("vector", ["scalar"],
                   lambda v, c=c: v.tensor_mul(tA[:, 0:3], xh[:, c, :], rstd_b[:, 0:3]))
            p.step("vector", [],
                   lambda v, c=c: v.tensor_add(xh_bf[:, c, :], tA[:, 0:3], nmr_b[:, 0:3]))



def _build_A_conv(p, L):
    nc = L["nc"]; tB = L["tB"]; tC = L["tC"]; tD = L["tD"]
    xh_bf = L["xh_bf"]; xn = L["xn"]; xe = L["xe"]; cw = L["cw"]; cb = L["cb"]
    xcb = L["xcb"]; xprojT_bf = L["xprojT_bf"]; ps = L["ps"]
    dt_bf = L["dt_bf"]; bc_f = L["bc_f"]; bc_bf = L["bc_bf"]
    if True:
        acc = [tB, tC]
        for c in range(NCH):
            par = c % 2
            def conv_cp(v, c=c):
                v.tensor_copy(xe[:, 0:3], xh_bf[:, c, :])
                return v.tensor_copy(xe[:, 3:TC + 3], xn[:, c, :])
            p.step("vector", ["scalar"], conv_cp)
            p.step("vector", [],
                   lambda v, c=c, par=par: v.tensor_scalar(
                       acc[par][:], xe[:, 3:TC + 3], cw[:, c, 3:4], None, ALU.mult))
            for k in range(3):
                p.step("vector", [],
                       lambda v, c=c, par=par, k=k: v.scalar_tensor_tensor(
                           acc[par][:], xe[:, k:k + TC], cw[:, c, k:k + 1], acc[par][:],
                           ALU.mult, ALU.add))
            p.step("scalar", ["vector"],
                   lambda s, c=c, par=par: s.activation(xcb[:, c, :], acc[par][:],
                                                        AF.Silu, bias=cb[:, c:c + 1]))
            def xproj_mm(tn, c=c):
                for tt in range(2):
                    sl = slice(tt * 512, (tt + 1) * 512)
                    tn.matmul(ps[2 + tt][:], xprojT_bf[:, c, 0:128], xcb[:, c, sl],
                              start=(c == 0), stop=(c == NCH - 1), skip_group_check=True)
                    mm = tn.matmul(ps[4 + tt][0:32, :], xprojT_bf[:, c, 128:160], xcb[:, c, sl],
                                   start=(c == 0), stop=(c == NCH - 1), skip_group_check=True)
                return mm
            p.step("tensor", ["scalar"], xproj_mm)

        p.step("scalar", ["tensor"],
               lambda s: (s.copy(dt_bf[:, 0:512], ps[2][:]),
                          s.copy(dt_bf[:, 512:1024], ps[3][:]),
                          s.copy(bc_f[:, 0:512], ps[4][0:32, :]),
                          s.copy(bc_f[:, 512:1024], ps[5][0:32, :]),
                          s.copy(bc_bf[:, 0:512], ps[4][0:32, :]),
                          s.copy(bc_bf[:, 512:1024], ps[5][0:32, :]))[-1])



def _build_A_scan(p, L, stages=99):
    nc = L["nc"]; tB = L["tB"]; tC = L["tC"]; tD = L["tD"]; tE = L["tE"]; tA = L["tA"]
    buf0 = L["buf0"]; xcb = L["xcb"]; a_t = L["a_t"]; Asb = L["Asb"]
    dtprojT_bf = L["dtprojT_bf"]; dt_bf = L["dt_bf"]; dpb = L["dpb"]
    selB = L["selB"]; bc_bf = L["bc_bf"]; ps = L["ps"]; hfin = L["hfin"]
    if True:
        for c in range(NCH):
            def dtp_mm(tn, c=c):
                for tt in range(2):
                    mm = tn.matmul(ps[6 + tt][:], dtprojT_bf[:, c * 128:(c + 1) * 128],
                                   dt_bf[:, tt * 512:(tt + 1) * 512], start=True, stop=True)
                return mm
            p.step("tensor", ["scalar", "vector"], dtp_mm)
            p.step("scalar", ["tensor"],
                   lambda s, c=c: (s.activation(tB[:, 0:512], ps[6][:], AF.Identity, bias=dpb[:, c:c + 1]),
                                   s.activation(tB[:, 512:1024], ps[7][:], AF.Identity, bias=dpb[:, c:c + 1]))[-1])
            p.step("scalar", ["scalar"], lambda s: s.activation(tC[:], tB[:], AF.Abs))
            p.step("scalar", ["scalar"], lambda s: s.activation(tC[:], tC[:], AF.Exp, scale=-1.0))
            p.step("scalar", ["scalar"], lambda s: s.activation(tC[:], tC[:], AF.Ln, bias=1.0))
            p.step("vector", ["scalar"],
                   lambda v: v.tensor_scalar_max(tD[:], tB[:], 0.0))
            p.step("vector", [],
                   lambda v, c=c: v.tensor_add(buf0[:, c, :], tD[:], tC[:]))
            p.step("vector", ["vector"],
                   lambda v, c=c: v.tensor_mul(tD[:], buf0[:, c, :], xcb[:, c, :]))
            vbase = p.n("vector")
            rd_a = [vbase, vbase]
            rd_ps = [vbase, vbase]
            for n in range(NST):
                parn = n % 2
                pset = n % 2
                b0, b1 = (0, 1) if pset == 0 else (2, 3)
                p.step("scalar", [("vector", rd_a[parn])],
                       lambda s, c=c, n=n, parn=parn:
                       s.activation(a_t[parn][:], buf0[:, c, :], AF.Exp,
                                    scale=Asb[:, c, n:n + 1]))
                p.step("tensor", [("vector", rd_ps[pset]), "gpsimd", ("scalar", p.n("scalar"))],
                       lambda tn, n=n, b0=b0, b1=b1: (
                           tn.matmul(ps[b0][:], selB[:, n, :], bc_bf[:, 0:512], start=True, stop=True),
                           tn.matmul(ps[b1][:], selB[:, n, :], bc_bf[:, 512:1024], start=True, stop=True))[-1])
                def bmul_v(v, c=c, n=n, b0=b0, b1=b1):
                    v.tensor_mul(tE[:, 0:512], tD[:, 0:512], ps[b0][:])
                    return v.tensor_mul(tE[:, 512:1024], tD[:, 512:1024], ps[b1][:])
                p.step("vector", ["scalar", "tensor"], bmul_v)
                p.step("vector", [],
                       lambda v, parn=parn: v.tensor_tensor_scan(
                           tA[:], a_t[parn][:], tE[:], 0.0, ALU.mult, ALU.add))
                rd_a[parn] = p.n("vector")
                rd_ps[pset] = p.n("vector")
                p.step("vector", ["vector"],
                       lambda v, c=c, n=n: v.tensor_copy(hfin[:, c * NST + n:c * NST + n + 1],
                                                         tA[:, TC - 1:TC]))



def _build_A_stores(p, L, stages):
    o_xn = L["o_xn"]; o_xc = L["o_xc"]; o_dl = L["o_dl"]; o_bc = L["o_bc"]
    o_hf = L["o_hf"]
    xn = L["xn"]; xcb = L["xcb"]; buf0 = L["buf0"]; bc_f = L["bc_f"]
    hfin = L["hfin"]
    if True:
        for f in [
            lambda s: s.dma_start(o_xn.ap().rearrange("(c p) t -> p c t", p=128), xn[:]),
            lambda s: s.dma_start(o_xc.ap().rearrange("(c p) t -> p c t", p=128), xcb[:]),
            lambda s: s.dma_start(o_dl.ap().rearrange("(c p) t -> p c t", p=128), buf0[:]),
            lambda s: s.dma_start(o_bc.ap(), bc_f[:]),
            lambda s: s.dma_start(o_hf.ap(), hfin[:]),
        ]:
            p.step("sync", ["vector", "scalar", "tensor"], f, inc=16)
        p.step("sync", ["sync"], lambda s: None, inc=0)




def build_B1(p_ext=None, IO=None, masked=False):
    if p_ext is not None:
        nc = p_ext.nc
    else:
        nc = bass.Bass("TRN2", target_bir_lowering=False)

    def T_in(name, shape, dtype):
        return IO[name] if IO is not None else nc.dram_tensor(name, shape, dtype, kind="ExternalInput")

    def T_out(name, shape, dtype):
        return IO[name] if IO is not None else nc.dram_tensor(name, shape, dtype, kind="ExternalOutput")
    i_dl = T_in("i_dl", [D, TC], F32)
    i_xc = T_in("i_xc", [D, TC], BF16)
    i_bc = T_in("i_bc", [32, TC], F32)
    A_log = T_in("A_log", [D, NST], F32)
    D_par = T_in("D_par", [128, NCH], F32)
    i_xn = T_in("i_xn", [D, TC], BF16)
    i_hi = T_in("i_hi", [128, NCH * NST], F32)
    o_s = T_out("o_s", [D, TC], F32)

    with ExitStack() as ctx:
        e = ctx.enter_context
        dl = e(nc.sbuf_tensor(_nm("dl"), [128, NCH, TC], F32))
        xcb = e(nc.sbuf_tensor(_nm("xcb"), [128, NCH, TC], BF16))
        sbuf = e(nc.sbuf_tensor(_nm("sbuf_s"), [128, NCH, TC], F32))
        tA = e(nc.sbuf_tensor(_nm("tA"), [128, TC], F32))
        tB = e(nc.sbuf_tensor(_nm("tB"), [128, TC], F32))
        tC = e(nc.sbuf_tensor(_nm("tC"), [128, TC], F32))
        tD = e(nc.sbuf_tensor(_nm("tD"), [128, TC], F32))
        tE = e(nc.sbuf_tensor(_nm("tE"), [128, TC], F32))
        bc_bf = e(nc.sbuf_tensor(_nm("bc_bf"), [32, TC], BF16))
        selB = e(nc.sbuf_tensor(_nm("selB"), [32, NST, 128], BF16))
        selC = e(nc.sbuf_tensor(_nm("selC"), [32, NST, 128], BF16))
        Asb = e(nc.sbuf_tensor(_nm("Asb"), [128, NCH, NST], F32))
        Dp = e(nc.sbuf_tensor(_nm("Dp"), [128, NCH], F32))
        hinit = e(nc.sbuf_tensor(_nm("hinit"), [128, NCH * NST], F32))
        xnp = [e(nc.sbuf_tensor(_nm(f"xnp{i}"), [128, TC], BF16)) for i in range(2)]
        ps = [e(nc.psum_tensor(_nm(f"ps{i}"), [128, 512], F32)) for i in range(4)]

        p = p_ext if p_ext is not None else Prog(nc, ctx)
        if p_ext is not None:
            p.fence()
        a_t = [tB, tC]

        for f in [
            lambda s: s.dma_start(dl[:], i_dl.ap().rearrange("(c p) t -> p c t", p=128)),
            lambda s: s.dma_start(xcb[:], i_xc.ap().rearrange("(c p) t -> p c t", p=128)),
            lambda s: s.dma_start(Asb[:], A_log.ap().rearrange("(c p) n -> p c n", p=128)),
            lambda s: s.dma_start(Dp[:], D_par.ap()),
            lambda s: s.dma_start(hinit[:], i_hi.ap()),
        ]:
            p.step("sync", [], f, inc=16)
        p.step("gpsimd", [], lambda g: g.dma_start(bc_bf[:], i_bc.ap()), inc=16)

        def consts(g):
            g.memset(selB[:], 0.0)
            return g.memset(selC[:], 0.0)
        p.step("gpsimd", [], consts)
        p.step("gpsimd", [], lambda g: g.affine_select(
            out=selB[:], in_=selB[:], compare_op=ALU.not_equal, fill=1.0, base=0,
            pattern=[[-1, NST], [0, 128]], channel_multiplier=1))
        p.step("gpsimd", [], lambda g: g.affine_select(
            out=selC[:], in_=selC[:], compare_op=ALU.not_equal, fill=1.0, base=-16,
            pattern=[[-1, NST], [0, 128]], channel_multiplier=1))

        p.step("scalar", ["sync"], lambda s: s.activation(Asb[:], Asb[:], AF.Exp))
        p.step("scalar", [], lambda s: s.mul(Asb[:], Asb[:], -1.0))
        p.step("vector", ["sync", "gpsimd", "scalar"], lambda v: None, inc=0)
        if masked:
            mskh = T_in("msk", [128, 1], F32)
            mskcol = e(nc.sbuf_tensor(_nm("mskcol"), [128, 1], F32))
            p.step("sync", [], lambda s: s.dma_start(mskcol[:], mskh.ap()), inc=16)
            p.step("vector", ["sync"], lambda v: v.tensor_scalar(
                hinit[:], hinit[:], mskcol[:], None, ALU.mult))

        for c in range(NCH):
            p.step("vector", [],
                   lambda v, c=c: v.tensor_mul(tD[:], dl[:, c, :], xcb[:, c, :]))
            for n in range(NST):
                parn = n % 2
                p.step("scalar", ["vector"],
                       lambda s, c=c, n=n, parn=parn:
                       s.activation(a_t[parn][:], dl[:, c, :], AF.Exp,
                                    scale=Asb[:, c, n:n + 1]))
                p.step("tensor", ["vector", "gpsimd", "scalar"],
                       lambda tn, n=n: (
                           tn.matmul(ps[0][:], selB[:, n, :], bc_bf[:, 0:512], start=True, stop=True),
                           tn.matmul(ps[1][:], selB[:, n, :], bc_bf[:, 512:1024], start=True, stop=True),
                           tn.matmul(ps[2][:], selC[:, n, :], bc_bf[:, 0:512], start=True, stop=True),
                           tn.matmul(ps[3][:], selC[:, n, :], bc_bf[:, 512:1024], start=True, stop=True))[-1])
                def bmul_v(v):
                    v.tensor_mul(tE[:, 0:512], tD[:, 0:512], ps[0][:])
                    return v.tensor_mul(tE[:, 512:1024], tD[:, 512:1024], ps[1][:])
                p.step("vector", ["scalar", "tensor"], bmul_v)
                idx = c * NST + n
                p.step("vector", [],
                       lambda v, parn=parn, idx=idx: v.tensor_tensor_scan(
                           tA[:], a_t[parn][:], tE[:], hinit[:, idx:idx + 1],
                           ALU.mult, ALU.add))
                def ymul_v(v):
                    v.tensor_mul(tE[:, 0:512], tA[:, 0:512], ps[2][:])
                    return v.tensor_mul(tE[:, 512:1024], tA[:, 512:1024], ps[3][:])
                p.step("vector", [], ymul_v)
                if n == 0:
                    p.step("vector", [],
                           lambda v, c=c: v.tensor_copy(sbuf[:, c, :], tE[:]))
                else:
                    p.step("vector", [],
                           lambda v, c=c: v.tensor_add(sbuf[:, c, :], sbuf[:, c, :], tE[:]))

        for c in range(NCH):
            par = c % 2
            xt = xnp[par]
            p.step("sync", ["scalar", "vector"],
                   lambda s, c=c, xt=xt: s.dma_start(
                       xt[:], i_xn.ap().rearrange("(c p) t -> p c t", p=128)[:, c, :]),
                   inc=16)
            p.step("vector", ["sync"],
                   lambda v, c=c, xt=xt: v.scalar_tensor_tensor(
                       sbuf[:, c, :], xt[:], Dp[:, c:c + 1], sbuf[:, c, :],
                       ALU.mult, ALU.add))

        p.step("sync", ["vector"],
               lambda s: s.dma_start(o_s.ap().rearrange("(c p) t -> p c t", p=128), sbuf[:]),
               inc=16)
        p.step("sync", ["sync"], lambda s: None, inc=0)

        if p_ext is None:
            block = e(nc.Block())
            p.emit(block)
    return nc


def _prep_B1_inmaps(inputs, resA):
    A_log = np.ascontiguousarray(inputs["L0_attn_A_log"])
    D_par = np.ascontiguousarray(inputs["L0_attn_D_param"].reshape(NCH, 128).T)
    x = inputs["L0_attn_x"]
    zeros_h = np.zeros((128, NCH * NST), np.float32)
    maps = []
    for c in range(8):
        b, h = c // 2, c % 2
        t0 = h * TC
        hi = zeros_h if h == 0 else np.ascontiguousarray(resA[c - 1]["o_hf"])
        maps.append(dict(i_dl=resA[c]["o_dl"], i_xc=resA[c]["o_xc"],
                         i_bc=resA[c]["o_bc"], A_log=A_log, D_par=D_par,
                         i_xn=resA[c]["o_xn"], i_hi=hi))
    return maps


def run_B1(inputs, resA):
    nc = build_B1()
    res = run_bass_kernel_spmd(nc, _prep_B1_inmaps(inputs, resA), core_ids=list(range(8)))
    return res.results




def build_B2(p_ext=None, IO=None):
    if p_ext is not None:
        nc = p_ext.nc
    else:
        nc = bass.Bass("TRN2", target_bir_lowering=False)

    def T_in(name, shape, dtype):
        return IO[name] if IO is not None else nc.dram_tensor(name, shape, dtype, kind="ExternalInput")

    def T_out(name, shape, dtype):
        return IO[name] if IO is not None else nc.dram_tensor(name, shape, dtype, kind="ExternalOutput")

    i_xn = T_in("i_xn", [D, TC], BF16)
    i_s = T_in("i_s", [D, TC], F32)
    xT = T_in("xT", [D, TC], F32)
    w1_all = T_in("w1_all", [NEXP, D, DFF], F32)
    w2_all = T_in("w2_all", [NEXP, DFF, D], F32)
    b1_all = T_in("b1_all", [128, NEXP, NFC], F32)
    sh_b2 = T_in("sh_b2", [128, 2, NCH], F32)
    rt_b2 = T_in("rt_b2", [16, D], F32)
    cenT = T_in("cenT", [D, 16], F32)
    cen_bias = T_in("cen_bias", [1, 16], F32)
    o_x1 = T_out("o_x1", [D, TC], F32)

    with ExitStack() as ctx:
        e = ctx.enter_context
        xn = e(nc.sbuf_tensor(_nm("xn"), [128, NCH, TC], BF16))
        y_sb = e(nc.sbuf_tensor(_nm("y_sb"), [128, NCH, 512], F32))
        gt_bf = e(nc.sbuf_tensor(_nm("gt_bf"), [32, TC], BF16))
        g_bf = e(nc.sbuf_tensor(_nm("g_bf"), [128, TC], BF16))
        selB = e(nc.sbuf_tensor(_nm("selB"), [32, NST, 128], BF16))
        cen_sb = e(nc.sbuf_tensor(_nm("cen_sb"), [128, NCH, 16], BF16))
        b1sb = e(nc.sbuf_tensor(_nm("b1sb"), [128, NEXP, NFC], F32))
        shb2 = e(nc.sbuf_tensor(_nm("shb2"), [128, 2, NCH], F32))
        shb2s = e(nc.sbuf_tensor(_nm("shb2s"), [128, NCH], F32))
        rtb2_bf = e(nc.sbuf_tensor(_nm("rtb2_bf"), [16, NCH, 128], BF16))
        ident_f = e(nc.sbuf_tensor(_nm("ident_f"), [128, 128], F32))
        ones_rf = e(nc.sbuf_tensor(_nm("ones_rf"), [1, 128], F32))
        bias_b = e(nc.sbuf_tensor(_nm("bias_b"), [128, 16], F32))
        sc_f = e(nc.sbuf_tensor(_nm("sc_f"), [128, 16], F32))
        sc_c = e(nc.sbuf_tensor(_nm("sc_c"), [128, 16], F32))
        ge_f = e(nc.sbuf_tensor(_nm("ge_f"), [128, 16], F32))
        m_f = e(nc.sbuf_tensor(_nm("m_f"), [128, 1], F32))
        sum_f = e(nc.sbuf_tensor(_nm("sum_f"), [128, 1], F32))
        g_tok = e(nc.sbuf_tensor(_nm("g_tok"), [128, 16], F32))
        x_t = e(nc.sbuf_tensor(_nm("x_t"), [128, TC], F32))
        s_t = e(nc.sbuf_tensor(_nm("s_t"), [128, TC], F32))
        g_t = e(nc.sbuf_tensor(_nm("g_t"), [128, TC], F32))
        w1g = [e(nc.sbuf_tensor(_nm(f"w1g{i}"), [128, NCH, 768], BF16)) for i in range(2)]
        w2sb = [e(nc.sbuf_tensor(_nm(f"w2sb{i}"), [128, NFC, D], BF16)) for i in range(1)]
        h1 = e(nc.sbuf_tensor(_nm("h1"), [128, NFC, 512], BF16))
        cbrow = e(nc.sbuf_tensor(_nm("cbrow"), [1, 16], F32))
        ps = [e(nc.psum_tensor(_nm(f"ps{i}"), [128, 512], F32)) for i in range(8)]

        p = p_ext if p_ext is not None else Prog(nc, ctx)
        if p_ext is not None:
            p.fence()

        for f in [
            lambda s: s.dma_start(xn[:], i_xn.ap().rearrange("(c p) t -> p c t", p=128)),
            lambda s: s.dma_start(b1sb[:], b1_all.ap()),
            lambda s: s.dma_start(shb2[:], sh_b2.ap()),
            lambda s: s.dma_start(cbrow[:], cen_bias.ap()),
        ]:
            p.step("sync", [], f, inc=16)
        p.step("gpsimd", [], lambda g: g.dma_start(
            cen_sb[:], cenT.ap().rearrange("(c p) j -> p c j", p=128)), inc=16)
        p.step("gpsimd", [], lambda g: g.dma_start(
            rtb2_bf[:], rt_b2.ap().rearrange("j (c p) -> j c p", p=128)), inc=16)

        def consts(g):
            g.memset(selB[:], 0.0)
            g.memset(gt_bf[:], 0.0)
            g.memset(ident_f[:], 0.0)
            return g.memset(ones_rf[:], 1.0)
        p.step("gpsimd", ["gpsimd"], consts)
        p.step("gpsimd", ["gpsimd"], lambda g: g.affine_select(
            out=selB[:], in_=selB[:], compare_op=ALU.not_equal, fill=1.0, base=0,
            pattern=[[-1, NST], [0, 128]], channel_multiplier=1))
        p.step("gpsimd", ["gpsimd"], lambda g: g.affine_select(
            out=ident_f[:], in_=ident_f[:], compare_op=ALU.not_equal, fill=1.0, base=0,
            pattern=[[-1, 128]], channel_multiplier=1))
        p.step("tensor", ["sync", "gpsimd"],
               lambda tn: tn.matmul(ps[1][:, 0:16], ones_rf[:], cbrow[:], start=True, stop=True))
        p.step("scalar", ["tensor"], lambda s: s.copy(bias_b[:], ps[1][:, 0:16]))

        for tile in range(8):
            tsl = slice(tile * 128, (tile + 1) * 128)
            def sc_mm(tn, tsl=tsl):
                for c in range(NCH):
                    mm = tn.matmul(ps[0][:, 0:16], xn[:, c, tsl], cen_sb[:, c, :],
                                   start=(c == 0), stop=(c == NCH - 1))
                return mm
            p.step("tensor", ["sync", "gpsimd", ("scalar", p.n("scalar"))], sc_mm)
            p.step("scalar", ["tensor"], lambda s: s.activation(sc_f[:], ps[0][:, 0:16], AF.Sigmoid))
            p.step("vector", ["scalar"], lambda v: v.tensor_add(sc_f[:], sc_f[:], bias_b[:]))
            p.step("vector", [], lambda v: v.tensor_copy(sc_c[:], sc_f[:]))
            for i in range(4):
                p.step("vector", [], lambda v: v.reduce_max(m_f[:], sc_c[:], axis=AX.X))
                if i < 3:
                    p.step("vector", [], lambda v: v.tensor_scalar(ge_f[:], sc_c[:], m_f[:], None, ALU.is_ge))
                    p.step("vector", [], lambda v: v.scalar_tensor_tensor(
                        sc_c[:], ge_f[:], -3.0e38, sc_c[:], ALU.mult, ALU.add))
            p.step("vector", [], lambda v: v.tensor_scalar(ge_f[:], sc_f[:], m_f[:], None, ALU.is_ge))
            p.step("vector", [], lambda v: v.tensor_mul(g_tok[:], ge_f[:], sc_f[:]))
            p.step("vector", [], lambda v: v.reduce_sum(sum_f[:], g_tok[:], axis=AX.X))
            p.step("vector", [], lambda v: v.tensor_scalar_max(sum_f[:], sum_f[:], 1e-8))
            p.step("vector", [], lambda v: v.reciprocal(m_f[:], sum_f[:]))
            p.step("vector", [], lambda v: v.tensor_scalar(g_tok[:], g_tok[:], m_f[:], None, ALU.mult))
            p.step("tensor", ["vector", ("scalar", p.n("scalar"))],
                   lambda tn: tn.transpose(ps[1][0:16, 0:128], g_tok[:], ident_f[:]))
            p.step("scalar", ["tensor"],
                   lambda s, tsl=tsl: s.copy(gt_bf[0:16, tsl], ps[1][0:16, 0:128]))

        GR2 = [(0, 6), (6, 5)]
        w1_tasks = [(hf, j, gi) for hf in range(2) for j in range(NEXP) for gi in range(2)]
        w1_after = {}
        w1_reader = {}
        w2_after = {}
        w2_reader = {}

        def w1_dma(k):
            hf, j, gi = w1_tasks[k]
            f0, nf = GR2[gi]
            par = k % 2
            waits = []
            if k >= 2:
                waits.append(("tensor", w1_reader[k - 2]))
            p.step("gpsimd", waits,
                   lambda g, j=j, f0=f0, nf=nf, par=par: g.dma_start(
                       w1g[par][:, :, 0:nf * 128],
                       w1_all.ap()[j, :, f0 * 128:(f0 + nf) * 128]
                       .rearrange("(c p) f -> p c f", p=128)),
                   inc=16)
            w1_after[k] = p.n("gpsimd")

        def w2_dma(hf, j, rd=None):
            waits = [] if rd is None else [("tensor", rd)]
            p.step("gpsimd", waits,
                   lambda g, j=j: g.dma_start(
                       w2sb[0][:], w2_all.ap()[j].rearrange("(f p) d -> p f d", p=128)),
                   inc=16)
            w2_after[(hf, j)] = p.n("gpsimd")

        w1_dma(0)
        w1_dma(1)
        w2_dma(0, 0)

        for hf in range(2):
            hsl = slice(hf * 512, (hf + 1) * 512)
            for j in range(NEXP):
                k0 = (hf * NEXP + j) * 2
                for gi, (f0, nf) in enumerate(GR2):
                    k = k0 + gi
                    par = k % 2
                    def h1_mm(tn, f0=f0, nf=nf, par=par, hsl=hsl):
                        for c in range(NCH):
                            for q in range(nf):
                                mm = tn.matmul(ps[q][:], w1g[par][:, c, q * 128:(q + 1) * 128],
                                               xn[:, c, hsl], start=(c == 0),
                                               stop=(c == NCH - 1), skip_group_check=True)
                        return mm
                    p.step("tensor", [("gpsimd", w1_after[k]),
                                      ("scalar", p.n("scalar"))], h1_mm)
                    w1_reader[k] = p.n("tensor")
                    if k + 2 < len(w1_tasks):
                        w1_dma(k + 2)
                    def h1_ev(s, j=j, f0=f0, nf=nf):
                        for q in range(nf):
                            ev = s.activation(h1[:, f0 + q, :], ps[q][:], AF.Relu,
                                              bias=b1sb[:, j, f0 + q:f0 + q + 1])
                        return ev
                    p.step("scalar", ["tensor"], h1_ev)
                if j >= 2:
                    def gb_mm(tn, j=j, hsl=hsl):
                        return tn.matmul(ps[6][:], selB[:, j - 2, :], gt_bf[:, hsl],
                                         start=True, stop=True)
                    p.step("tensor", ["vector", ("scalar", p.n("scalar"))], gb_mm)
                    p.step("scalar", ["tensor"],
                           lambda s: s.copy(g_bf[:, 0:512], ps[6][:]))
                    for q in range(NFC):
                        p.step("vector", ["scalar"],
                               lambda v, q=q: v.tensor_mul(h1[:, q, :], h1[:, q, :], g_bf[:, 0:512]))
                for c in range(NCH):
                    pary = c % 2
                    def y_mm(tn, c=c, pary=pary):
                        for q in range(NFC):
                            mm = tn.matmul(ps[6 + pary][:], w2sb[0][:, q, c * 128:(c + 1) * 128],
                                           h1[:, q, :], start=(q == 0), stop=(q == NFC - 1))
                        return mm
                    p.step("tensor", [("gpsimd", w2_after[(hf, j)]), "vector",
                                      ("scalar", p.n("scalar"))], y_mm)
                    if j == 0:
                        p.step("vector", ["tensor"],
                               lambda v, c=c, pary=pary: v.tensor_copy(
                                   y_sb[:, c, :], ps[6 + pary][:]))
                    else:
                        p.step("vector", ["tensor"],
                               lambda v, c=c, pary=pary: v.tensor_add(
                                   y_sb[:, c, :], y_sb[:, c, :], ps[6 + pary][:]))
                w2_reader[(hf, j)] = p.n("tensor")
                if j + 1 < NEXP:
                    w2_dma(hf, j + 1, rd=w2_reader[(hf, j)])
                elif hf == 0:
                    w2_dma(1, 0, rd=w2_reader[(hf, j)])

            if hf == 0:
                p.step("vector", ["sync"], lambda v: v.tensor_add(
                    shb2s[:], shb2[:, 0, :], shb2[:, 1, :]))
            for c in range(NCH):
                pary = c % 2
                p.step("tensor", ["vector"],
                       lambda tn, c=c, pary=pary, hsl=hsl: tn.matmul(
                           ps[6 + pary][:], rtb2_bf[:, c, :], gt_bf[0:16, hsl],
                           start=True, stop=True))
                p.step("vector", ["tensor"],
                       lambda v, c=c, pary=pary: v.tensor_add(
                           y_sb[:, c, :], y_sb[:, c, :], ps[6 + pary][:]))
            for c in range(NCH):
                p.step("sync", [("vector", p.n("vector")), ("scalar", p.n("scalar"))],
                       lambda s, c=c, hsl=hsl: s.dma_start(
                           x_t[:, 0:512],
                           xT.ap().rearrange("(c p) t -> p c t", p=128)[:, c, hsl]),
                       inc=16)
                p.step("sync", [],
                       lambda s, c=c, hsl=hsl: s.dma_start(
                           s_t[:, 0:512],
                           i_s.ap().rearrange("(c p) t -> p c t", p=128)[:, c, hsl]),
                       inc=16)
                p.step("vector", ["sync"],
                       lambda v, c=c, hsl=hsl: v.tensor_add(
                           y_sb[:, c, :], y_sb[:, c, :], xn[:, c, hsl]))
                p.step("vector", [],
                       lambda v, c=c: v.tensor_scalar(y_sb[:, c, :], y_sb[:, c, :],
                                                      shb2s[:, c:c + 1], None, ALU.add))
                p.step("scalar", ["vector", ("sync", p.n("sync"))],
                       lambda s, c=c: s.activation(g_t[:, 0:512], y_sb[:, c, :], AF.Silu))
                p.step("vector", ["scalar"],
                       lambda v: v.tensor_mul(g_t[:, 0:512], g_t[:, 0:512], s_t[:, 0:512]))
                p.step("vector", [], lambda v: v.tensor_add(g_t[:, 0:512], g_t[:, 0:512], x_t[:, 0:512]))
                p.step("sync", ["vector"],
                       lambda s, c=c, hsl=hsl: s.dma_start(
                           o_x1.ap().rearrange("(c p) t -> p c t", p=128)[:, c, hsl],
                           g_t[:, 0:512]),
                       inc=16)
        p.step("sync", ["sync"], lambda s: None, inc=0)

        if p_ext is None:
            block = e(nc.Block())
            p.emit(block)
    return nc


def _prep_B2_inmaps(inputs, resA, resB1):
    w1_all = np.ascontiguousarray(np.concatenate(
        [inputs["L0_ffn_shared_w1"], inputs["L0_ffn_routed_w1"]], 0))
    w2_all = np.ascontiguousarray(np.concatenate(
        [inputs["L0_ffn_shared_w2"], inputs["L0_ffn_routed_w2"]], 0))
    b1_all = np.ascontiguousarray(np.concatenate(
        [inputs["L0_ffn_shared_b1"], inputs["L0_ffn_routed_b1"]], 0)
        .reshape(NEXP, NFC, 128).transpose(2, 0, 1))
    sh_b2 = np.ascontiguousarray(
        inputs["L0_ffn_shared_b2"].reshape(2, NCH, 128).transpose(2, 0, 1))
    rt_b2 = np.ascontiguousarray(inputs["L0_ffn_routed_b2"])
    cenT = np.ascontiguousarray(inputs["L0_ffn_centroids"].T)
    cen_bias = np.ascontiguousarray(inputs["L0_ffn_bias"].reshape(1, 16))
    x = inputs["L0_attn_x"]
    maps = []
    for c in range(8):
        b, h = c // 2, c % 2
        t0 = h * TC
        maps.append(dict(i_xn=resA[c]["o_xn"], i_s=resB1[c]["o_s"],
                         xT=_fm(x[b, t0:t0 + TC]),
                         w1_all=w1_all, w2_all=w2_all, b1_all=b1_all,
                         sh_b2=sh_b2, rt_b2=rt_b2, cenT=cenT, cen_bias=cen_bias))
    return maps


def run_B2(inputs, resA, resB1):
    nc = build_B2()
    res = run_bass_kernel_spmd(nc, _prep_B2_inmaps(inputs, resA, resB1), core_ids=list(range(8)))
    return res.results




def build_C(p_ext=None, IO=None):
    if p_ext is not None:
        nc = p_ext.nc
    else:
        nc = bass.Bass("TRN2", target_bir_lowering=False)

    def T_in(name, shape, dtype):
        return IO[name] if IO is not None else nc.dram_tensor(name, shape, dtype, kind="ExternalInput")

    def T_out(name, shape, dtype):
        return IO[name] if IO is not None else nc.dram_tensor(name, shape, dtype, kind="ExternalOutput")
    x1a = T_in("x1a", [D, TC], F32)
    x1b = T_in("x1b", [D, TC], F32)
    wqT = T_in("wqT", [D, NH * HD], F32)
    wkT = T_in("wkT", [D, NKV * HD], F32)
    wvT = T_in("wvT", [D, NKV * HD], F32)
    o_y = T_out("o_y", [NH * HD, T], BF16)

    LOG1E4 = float(np.log(10000.0))
    TWO_PI = float(2 * np.pi)

    with ExitStack() as ctx:
        e = ctx.enter_context
        xn1 = e(nc.sbuf_tensor(_nm("xn1"), [128, NCH, T], BF16))
        q_fm = e(nc.sbuf_tensor(_nm("q_fm"), [128, 8, T], BF16))
        k_fm = e(nc.sbuf_tensor(_nm("k_fm"), [128, 2, T], BF16))
        v_aug = e(nc.sbuf_tensor(_nm("v_aug"), [128, NST, NKV * 65], BF16))
        cos_q = e(nc.sbuf_tensor(_nm("cos_q"), [128, NST, 64], BF16))
        sin_q = e(nc.sbuf_tensor(_nm("sin_q"), [128, NST, 64], BF16))
        cos_k = e(nc.sbuf_tensor(_nm("cos_k"), [128, NST, 64], BF16))
        sin_k = e(nc.sbuf_tensor(_nm("sin_k"), [128, NST, 64], BF16))
        rstd_b = e(nc.sbuf_tensor(_nm("rstd_b"), [128, T], F32))
        nmr_b = e(nc.sbuf_tensor(_nm("nmr_b"), [128, T], F32))
        rm = e(nc.sbuf_tensor(_nm("rm"), [1, 512], F32))
        rv = e(nc.sbuf_tensor(_nm("rv"), [1, 512], F32))
        rn = e(nc.sbuf_tensor(_nm("rn"), [1, 512], F32))
        sq5 = e(nc.sbuf_tensor(_nm("sq5"), [128, 512], F32))
        xc_t = [e(nc.sbuf_tensor(_nm("xc_t0"), [128, T], F32))] * 2
        wq_sb = e(nc.sbuf_tensor(_nm("wq_sb"), [128, NCH, 512], BF16))
        wk_sb = e(nc.sbuf_tensor(_nm("wk_sb"), [128, NCH, NKV * HD], BF16))
        wv_sb = e(nc.sbuf_tensor(_nm("wv_sb"), [128, NCH, NKV * HD], BF16))
        probs = [e(nc.sbuf_tensor(_nm(f"probs{i}"), [128, 512], BF16)) for i in range(2)]
        M0 = e(nc.sbuf_tensor(_nm("M0"), [128, 896], BF16))
        y_f32 = e(nc.sbuf_tensor(_nm("y_f32"), [65, 512], F32))
        ybuf = e(nc.sbuf_tensor(_nm("ybuf"), [64, 512], BF16))
        q_raw = e(nc.sbuf_tensor(_nm("q_raw"), [128, 512], F32))
        rope1 = e(nc.sbuf_tensor(_nm("rope1"), [128, 512], F32))
        rope2 = e(nc.sbuf_tensor(_nm("rope2"), [128, 512], F32))
        q_tok = e(nc.sbuf_tensor(_nm("q_tok"), [128, 512], BF16))
        fr = [e(nc.sbuf_tensor(_nm(f"fr{i}"), [128, 32], F32)) for i in range(4)]
        fri = e(nc.sbuf_tensor(_nm("fri"), [128, 32], I32))
        t_row = e(nc.sbuf_tensor(_nm("t_row"), [1, 128], F32))
        th_row = e(nc.sbuf_tensor(_nm("th_row"), [1, 32], F32))
        ones_all = e(nc.sbuf_tensor(_nm("ones_all"), [128, 128], F32))
        ident_f = e(nc.sbuf_tensor(_nm("ident_f"), [128, 128], F32))
        eps_col = e(nc.sbuf_tensor(_nm("eps_col"), [128, 1], F32))
        shift_col = e(nc.sbuf_tensor(_nm("shift_col"), [128, 1], F32))
        ps = [e(nc.psum_tensor(_nm(f"ps{i}"), [128, 512], F32)) for i in range(8)]

        p = p_ext if p_ext is not None else Prog(nc, ctx)
        if p_ext is not None:
            p.fence()
        rows = {"m": rm, "var": rv, "mm": rn, "rstd": rv, "nmr": rn}

        def consts(g):
            g.memset(ones_all[:], 1.0)
            g.memset(ident_f[:], 0.0)
            g.memset(eps_col[:], LN_EPS)
            g.memset(shift_col[:], EXP_SHIFT)
            g.memset(M0[:], 0.0)
            return g.memset(
                v_aug[:].rearrange("p t (k u) -> p t k u", u=65)[:, :, :, 64:65], 1.0)
        p.step("gpsimd", [], consts)
        p.step("gpsimd", [], lambda g: g.affine_select(
            out=ident_f[:], in_=ident_f[:], compare_op=ALU.not_equal, fill=1.0, base=0,
            pattern=[[-1, 128]], channel_multiplier=1))
        p.step("gpsimd", [], lambda g: g.affine_select(
            out=M0[:], in_=M0[:], compare_op=ALU.is_ge, fill=-1e30, base=-384,
            pattern=[[1, 896]], channel_multiplier=-1))
        p.step("gpsimd", [], lambda g: g.dma_start(
            wk_sb[:], wkT.ap().rearrange("(c p) f -> p c f", p=128)), inc=16)
        p.step("gpsimd", [], lambda g: g.dma_start(
            wv_sb[:], wvT.ap().rearrange("(c p) f -> p c f", p=128)), inc=16)
        p.step("gpsimd", [], lambda g: g.iota(
            th_row[:], pattern=[[1, 32]], base=0, channel_multiplier=0,
            allow_small_or_imprecise_dtypes=True))
        p.step("scalar", ["gpsimd"], lambda s: s.activation(
            th_row[:], th_row[:], AF.Exp, scale=-LOG1E4 / 32.0))

        for c in range(NCH):
            par = c % 2
            p.step("sync", [("tensor", p.n("tensor")), ("scalar", p.n("scalar"))],
                   lambda s, c=c, par=par: s.dma_start(
                       xc_t[par][:, 0:TC],
                       x1a.ap().rearrange("(c p) t -> p c t", p=128)[:, c, :]),
                   inc=16)
            p.step("sync", [],
                   lambda s, c=c, par=par: s.dma_start(
                       xc_t[par][:, TC:T],
                       x1b.ap().rearrange("(c p) t -> p c t", p=128)[:, c, :]),
                   inc=16)
            for tt in range(4):
                sl = slice(tt * 512, (tt + 1) * 512)
                p.step("scalar", ["sync", "tensor"],
                       lambda s, c=c, par=par, sl=sl: s.activation(
                           sq5[:], xc_t[par][:, sl], AF.Square))
                p.step("tensor", ["scalar", "sync"],
                       lambda tn, c=c, par=par, sl=sl, tt=tt: (
                           tn.matmul(ps[tt][0:1, :], ones_all[:, 0:1], xc_t[par][:, sl],
                                     start=(c == 0), stop=(c == NCH - 1),
                                     skip_group_check=True),
                           tn.matmul(ps[4 + tt][0:1, :], ones_all[:, 0:1], sq5[:],
                                     start=(c == 0), stop=(c == NCH - 1),
                                     skip_group_check=True))[-1])
        for tt in range(4):
            sl = slice(tt * 512, (tt + 1) * 512)
            def fin_v(v, tt=tt):
                v.tensor_scalar_mul(rm[:], ps[tt][0:1, :], 1.0 / D)
                return v.tensor_scalar_mul(rv[:], ps[4 + tt][0:1, :], 1.0 / D)
            p.step("vector", ["tensor"], fin_v)
            p.step("vector", [], lambda v: v.tensor_mul(rn[:], rm[:], rm[:]))
            p.step("vector", [], lambda v: v.tensor_sub(rv[:], rv[:], rn[:]))
            p.step("scalar", ["vector"], lambda s: s.activation(rv[:], rv[:], AF.Ln, bias=eps_col[0:1, :]))
            p.step("scalar", [], lambda s: s.activation(rv[:], rv[:], AF.Exp, scale=-0.5))
            p.step("vector", ["scalar"], lambda v: v.scalar_tensor_tensor(
                rn[:], rm[:], -1.0, rv[:], ALU.mult, ALU.mult))
            p.step("tensor", ["vector"],
                   lambda tn, tt=tt: (
                       tn.matmul(ps[tt][:], ones_all[0:1, :], rv[:], start=True, stop=True),
                       tn.matmul(ps[4 + tt][:], ones_all[0:1, :], rn[:], start=True, stop=True))[-1])
            p.step("scalar", ["tensor"],
                   lambda s, tt=tt, sl=sl: (
                       s.copy(rstd_b[:, sl], ps[tt][:]),
                       s.copy(nmr_b[:, sl], ps[4 + tt][:]))[-1])
        for c in range(NCH):
            par = c % 2
            p.step("sync", [("vector", p.n("vector"))],
                   lambda s, c=c, par=par: s.dma_start(
                       xc_t[par][:, 0:TC],
                       x1a.ap().rearrange("(c p) t -> p c t", p=128)[:, c, :]),
                   inc=16)
            p.step("sync", [],
                   lambda s, c=c, par=par: s.dma_start(
                       xc_t[par][:, TC:T],
                       x1b.ap().rearrange("(c p) t -> p c t", p=128)[:, c, :]),
                   inc=16)
            for tt4 in range(4):
                s4 = slice(tt4 * 512, (tt4 + 1) * 512)
                p.step("vector", ["sync", "scalar"],
                       lambda v, c=c, par=par, s4=s4: v.tensor_mul(
                           ps[5][:], xc_t[par][:, s4], rstd_b[:, s4]))
                p.step("vector", [],
                       lambda v, c=c, s4=s4: v.tensor_add(
                           xn1[:, c, s4], ps[5][:], nmr_b[:, s4]))

        for tile in range(NST):
            p.step("gpsimd", [("tensor", p.n("tensor"))], lambda g, tile=tile: g.iota(
                t_row[:], pattern=[[1, 128]], base=tile * 128, channel_multiplier=0,
                allow_small_or_imprecise_dtypes=True))
            p.step("tensor", ["gpsimd", "scalar", ("vector", p.n("vector"))],
                   lambda tn: tn.matmul(ps[2][:, 0:32], t_row[:], th_row[:],
                                        start=True, stop=True))
            p.step("scalar", ["tensor"], lambda s: s.copy(fr[0][:], ps[2][:, 0:32]))
            p.step("vector", ["scalar"], lambda v: v.tensor_scalar_mul(fr[1][:], fr[0][:], 1.0 / TWO_PI))
            p.step("vector", [], lambda v: v.tensor_copy(fri[:], fr[1][:]))
            p.step("vector", [], lambda v: v.tensor_copy(fr[1][:], fri[:]))
            p.step("vector", [], lambda v: v.scalar_tensor_tensor(
                fr[2][:], fr[1][:], -TWO_PI, fr[0][:], ALU.mult, ALU.add))
            p.step("scalar", ["vector"], lambda s: s.activation(fr[0][:], fr[2][:], AF.Sin))
            p.step("vector", ["scalar"], lambda v: v.tensor_scalar_add(
                fr[3][:], fr[2][:], float(np.pi / 2)))
            p.step("vector", [], lambda v: v.tensor_scalar(
                fr[1][:], fr[3][:], float(np.pi), None, ALU.is_gt))
            p.step("vector", [], lambda v: v.scalar_tensor_tensor(
                fr[3][:], fr[1][:], -TWO_PI, fr[3][:], ALU.mult, ALU.add))
            p.step("scalar", ["vector"], lambda s: s.activation(fr[1][:], fr[3][:], AF.Sin))
            def wtab(v, tile=tile):
                v.tensor_scalar_mul(cos_q[:, tile, 0:32], fr[1][:], 0.125)
                v.tensor_scalar_mul(cos_q[:, tile, 32:64], fr[1][:], 0.125)
                v.tensor_scalar_mul(sin_q[:, tile, 0:32], fr[0][:], -0.125)
                v.tensor_scalar_mul(sin_q[:, tile, 32:64], fr[0][:], 0.125)
                v.tensor_scalar_mul(cos_k[:, tile, 0:32], fr[1][:], 1.0)
                v.tensor_scalar_mul(cos_k[:, tile, 32:64], fr[1][:], 1.0)
                v.tensor_scalar_mul(sin_k[:, tile, 0:32], fr[0][:], -1.0)
                return v.tensor_scalar_mul(sin_k[:, tile, 32:64], fr[0][:], 1.0)
            p.step("vector", ["scalar"], wtab)

        def rep_view(tab, tile, nb):
            th = tab.tensor if hasattr(tab, "tensor") else tab
            return bass.AP(th, tile * 64, [[NST * 64, 128], [0, nb], [1, 64]])

        def swap_view(buf, nb):
            th = buf.tensor if hasattr(buf, "tensor") else buf
            return bass.AP(th, 32, [[512, 128], [64, nb], [-32, 2], [1, 32]])

        for tkt in range(NST):
            tsl = slice(tkt * 128, (tkt + 1) * 128)
            def kv_mm(tn, tsl=tsl):
                for c in range(NCH):
                    tn.matmul(ps[0][:, 0:NKV * HD], xn1[:, c, tsl], wk_sb[:, c, :],
                              start=(c == 0), stop=(c == NCH - 1), skip_group_check=True)
                    mm = tn.matmul(ps[1][:, 0:NKV * HD], xn1[:, c, tsl], wv_sb[:, c, :],
                                   start=(c == 0), stop=(c == NCH - 1), skip_group_check=True)
                return mm
            p.step("tensor", ["gpsimd", "scalar", ("vector", p.n("vector"))], kv_mm)
            p.step("scalar", ["tensor"], lambda s: s.copy(q_raw[:, 0:NKV * HD], ps[0][:, 0:NKV * HD]))
            def v_ev(s, tkt=tkt):
                for kv in range(NKV):
                    ev = s.copy(v_aug[:, tkt, kv * 65:kv * 65 + 64],
                                ps[1][:, kv * HD:(kv + 1) * HD])
                return ev
            p.step("scalar", [], v_ev)
            p.step("vector", ["scalar"],
                   lambda v, tkt=tkt: v.tensor_mul(
                       rope1[:, 0:256].rearrange("p (b u) -> p b u", b=NKV),
                       q_raw[:, 0:256].rearrange("p (b u) -> p b u", b=NKV),
                       rep_view(cos_k, tkt, NKV)))
            p.step("vector", [],
                   lambda v, tkt=tkt: v.tensor_mul(
                       rope2[:, 0:256].rearrange("p (b u) -> p b u", b=NKV),
                       swap_view(q_raw, NKV),
                       rep_view(sin_k, tkt, NKV)))
            p.step("vector", [],
                   lambda v: v.tensor_add(q_raw[:, 0:256], rope1[:, 0:256], rope2[:, 0:256]))
            for kv in range(NKV):
                p.step("tensor", ["vector", "scalar"],
                       lambda tn, kv=kv: tn.transpose(
                           ps[2][0:HD, 0:128],
                           q_raw[:, kv * HD:(kv + 1) * HD], ident_f[:]))
                p.step("scalar", ["tensor"],
                       lambda s, kv=kv, tsl=tsl: s.copy(
                           k_fm[64 * (kv % 2):64 * (kv % 2) + 64, kv // 2, tsl],
                           ps[2][0:HD, 0:128]))

        for ft in range(2):
            p.step("gpsimd", [("tensor", p.n("tensor"))],
                   lambda g, ft=ft: g.dma_start(
                       wq_sb[:], wqT.ap()[:, ft * 512:(ft + 1) * 512]
                       .rearrange("(c p) f -> p c f", p=128)),
                   inc=16)
            for tqt in range(NST):
                tsl = slice(tqt * 128, (tqt + 1) * 128)
                def q_mm(tn, tsl=tsl):
                    for c in range(NCH):
                        mm = tn.matmul(ps[0][:], xn1[:, c, tsl], wq_sb[:, c, :],
                                       start=(c == 0), stop=(c == NCH - 1))
                    return mm
                p.step("tensor", ["gpsimd", "scalar", ("vector", p.n("vector"))], q_mm)
                p.step("scalar", ["tensor"], lambda s: s.copy(q_raw[:], ps[0][:]))
                p.step("vector", ["scalar"],
                       lambda v, tqt=tqt: v.tensor_mul(
                           rope1[:].rearrange("p (b u) -> p b u", b=8),
                           q_raw[:].rearrange("p (b u) -> p b u", b=8),
                           rep_view(cos_q, tqt, 8)))
                p.step("vector", [],
                       lambda v, tqt=tqt: v.tensor_mul(
                           rope2[:].rearrange("p (b u) -> p b u", b=8),
                           swap_view(q_raw, 8),
                           rep_view(sin_q, tqt, 8)))
                p.step("vector", [],
                       lambda v: v.tensor_add(q_raw[:], rope1[:], rope2[:]))
                for hh in range(8):
                    h = ft * 8 + hh
                    qb = 64 * ((h // 4) % 2)
                    qs = (h % 4) + 4 * (h // 8)
                    p.step("tensor", ["vector", "scalar"],
                           lambda tn, hh=hh: tn.transpose(
                               ps[1][0:HD, 0:128],
                               q_raw[:, hh * HD:(hh + 1) * HD], ident_f[:]))
                    p.step("scalar", ["tensor"],
                           lambda s, qb=qb, qs=qs, tsl=tsl: s.copy(
                               q_fm[qb:qb + 64, qs, tsl],
                               ps[1][0:HD, 0:128]))

        for h in range(NH):
            kv = h // 4
            qb = 64 * ((h // 4) % 2)
            qs = (h % 4) + 4 * (h // 8)
            kb = 64 * (kv % 2)
            for tq4 in range(4):
                qsl = slice(tq4 * 512, (tq4 + 1) * 512)
                ntk = 4 * (tq4 + 1)
                for tkt in range(ntk):
                    par = tkt % 2
                    p.step("tensor", ["vector", "scalar"],
                           lambda tn, h=h, kv=kv, qb=qb, qs=qs, kb=kb, qsl=qsl, tkt=tkt, par=par:
                           tn.matmul(ps[4 + par][:],
                                     k_fm[kb:kb + 64, kv // 2, tkt * 128:(tkt + 1) * 128],
                                     q_fm[qb:qb + 64, qs, qsl],
                                     start=True, stop=True))
                    diag = tkt >= 4 * tq4
                    if diag:
                        off = 384 - (tkt * 128 - tq4 * 512)
                        p.step("vector", ["tensor"],
                               lambda v, par=par, off=off: v.tensor_add(
                                   ps[4 + par][:], ps[4 + par][:], M0[:, off:off + 512]))
                    p.step("scalar", ["tensor", "vector"],
                           lambda s, par=par: s.activation(
                               probs[par][:], ps[4 + par][:], AF.Exp, bias=shift_col[:]))
                    p.step("tensor", ["scalar"],
                           lambda tn, kv=kv, tkt=tkt, par=par, first=(tkt == 0), last=(tkt == ntk - 1):
                           tn.matmul(ps[3][0:65, :],
                                     v_aug[:, tkt, kv * 65:(kv + 1) * 65],
                                     probs[par][:], start=first, stop=last))
                p.step("scalar", ["tensor"], lambda s: s.copy(y_f32[:], ps[3][0:65, :]))
                p.step("vector", ["scalar"],
                       lambda v: v.reciprocal(y_f32[64:65, :], y_f32[64:65, :]))
                p.step("tensor", ["vector"],
                       lambda tn: tn.matmul(ps[6][0:64, :], ones_all[64:65, 0:64],
                                            y_f32[64:65, :], start=True, stop=True))
                p.step("vector", ["tensor", ("sync", p.n("sync"))],
                       lambda v: v.tensor_mul(ybuf[:], y_f32[0:64, :], ps[6][0:64, :]))
                p.step("sync", ["vector"],
                       lambda s, h=h, qsl=qsl: s.dma_start(
                           o_y.ap()[h * 64:(h + 1) * 64, qsl], ybuf[:]),
                       inc=16)
        p.step("sync", ["sync"], lambda s: None, inc=0)

        if p_ext is None:
            block = e(nc.Block())
            p.emit(block)
    return nc


_ROPE_PERM = np.concatenate([np.arange(0, HD, 2), np.arange(1, HD, 2)])


def _prep_C_inmaps(inputs, resB2):
    wq = inputs["L1_attn_wq"]
    wk = inputs["L1_attn_wk"]
    wv = inputs["L1_attn_wv"]
    maps = []
    for c in range(8):
        half = c % 2
        rows_q = np.concatenate([h * HD + _ROPE_PERM for h in range(half * NH, half * NH + NH)])
        rows_k = np.concatenate([h * HD + _ROPE_PERM for h in range(half * NKV, half * NKV + NKV)])
        rows_v = np.arange(half * NKV * HD, (half + 1) * NKV * HD)
        wqT_h = np.ascontiguousarray(wq[rows_q].T)
        wkT_h = np.ascontiguousarray(wk[rows_k].T)
        wvT_h = np.ascontiguousarray(wv[rows_v].T)
        pair = c // 2
        maps.append(dict(x1a=resB2[2 * pair]["o_x1"], x1b=resB2[2 * pair + 1]["o_x1"],
                         wqT=wqT_h, wkT=wkT_h, wvT=wvT_h))
    return maps


def run_C(inputs, resB2):
    nc = build_C()
    res = run_bass_kernel_spmd(nc, _prep_C_inmaps(inputs, resB2), core_ids=list(range(8)))
    return res.results




def build_D(p_ext=None, IO=None, split_y=False):
    if p_ext is not None:
        nc = p_ext.nc
    else:
        nc = bass.Bass("TRN2", target_bir_lowering=False)

    def T_in(name, shape, dtype):
        return IO[name] if IO is not None else nc.dram_tensor(name, shape, dtype, kind="ExternalInput")

    def T_out(name, shape, dtype):
        return IO[name] if IO is not None else nc.dram_tensor(name, shape, dtype, kind="ExternalOutput")
    y_full = None if split_y else T_in("y_full", [D, TC], BF16)
    x1T = T_in("x1T", [D, TC], F32)
    woT = T_in("woT", [D, D], F32)
    W1T = T_in("W1T", [D, MLPD], F32)
    W2T = T_in("W2T", [MLPD, MLPD], F32)
    W3T = T_in("W3T", [MLPD, D], F32)
    b1r = T_in("b1r", [128, 32], F32)
    b2r = T_in("b2r", [128, 32], F32)
    b3r = T_in("b3r", [128, 16], F32)
    o_out = T_out("o_out", [D, TC], F32)

    with ExitStack() as ctx:
        e = ctx.enter_context
        xn1 = e(nc.sbuf_tensor(_nm("xn1"), [128, NCH, TC], BF16))
        s1_bf = e(nc.sbuf_tensor(_nm("s1_bf"), [128, NCH, TC], BF16))
        rm = e(nc.sbuf_tensor(_nm("rm"), [1, 512], F32))
        rv = e(nc.sbuf_tensor(_nm("rv"), [1, 512], F32))
        rn = e(nc.sbuf_tensor(_nm("rn"), [1, 512], F32))
        rstd_b = e(nc.sbuf_tensor(_nm("rstd_b"), [128, TC], F32))
        nmr_b = e(nc.sbuf_tensor(_nm("nmr_b"), [128, TC], F32))
        sq5 = e(nc.sbuf_tensor(_nm("sq5"), [128, 512], F32))
        xc_t = e(nc.sbuf_tensor(_nm("xc_t"), [128, TC], F32))
        b1s = e(nc.sbuf_tensor(_nm("b1s"), [128, 32], F32))
        b2s = e(nc.sbuf_tensor(_nm("b2s"), [128, 32], F32))
        b3s = e(nc.sbuf_tensor(_nm("b3s"), [128, 16], F32))
        ones_all = e(nc.sbuf_tensor(_nm("ones_all"), [128, 128], F32))
        eps_col = e(nc.sbuf_tensor(_nm("eps_col"), [128, 1], F32))
        g_t = e(nc.sbuf_tensor(_nm("g_t"), [128, 512], F32))
        pr_t = e(nc.sbuf_tensor(_nm("pr_t"), [128, 512], F32))
        if split_y:
            msk0h = T_in("msk0", [128, 1], F32)
            msk1h = T_in("msk1", [128, 1], F32)
            m0c = e(nc.sbuf_tensor(_nm("m0c"), [128, 1], F32))
            m1c = e(nc.sbuf_tensor(_nm("m1c"), [128, 1], F32))
            ytmp0 = e(nc.sbuf_tensor(_nm("ytmp0"), [128, TC], BF16))
            ytmp1 = e(nc.sbuf_tensor(_nm("ytmp1"), [128, TC], BF16))
        ps = [e(nc.psum_tensor(_nm(f"ps{i}"), [128, 512], F32)) for i in range(8)]

        p = p_ext if p_ext is not None else Prog(nc, ctx)
        if p_ext is not None:
            p.fence()

        def consts(g):
            g.memset(ones_all[:], 1.0)
            return g.memset(eps_col[:], LN_EPS)
        p.step("gpsimd", [], consts)
        for f in [
            lambda s: s.dma_start(b1s[:], b1r.ap()),
            lambda s: s.dma_start(b2s[:], b2r.ap()),
            lambda s: s.dma_start(b3s[:], b3r.ap()),
        ]:
            p.step("sync", [], f, inc=16)

        for c in range(NCH):
            p.step("sync", [("tensor", p.n("tensor")), ("scalar", p.n("scalar"))],
                   lambda s, c=c: s.dma_start(
                       xc_t[:], x1T.ap().rearrange("(c p) t -> p c t", p=128)[:, c, :]),
                   inc=16)
            for tt in range(2):
                sl = slice(tt * 512, (tt + 1) * 512)
                p.step("scalar", ["sync", "tensor"],
                       lambda s, sl=sl: s.activation(sq5[:], xc_t[:, sl], AF.Square))
                p.step("tensor", ["scalar", "sync"],
                       lambda tn, c=c, sl=sl, tt=tt: (
                           tn.matmul(ps[tt][0:1, :], ones_all[:, 0:1], xc_t[:, sl],
                                     start=(c == 0), stop=(c == NCH - 1),
                                     skip_group_check=True),
                           tn.matmul(ps[2 + tt][0:1, :], ones_all[:, 0:1], sq5[:],
                                     start=(c == 0), stop=(c == NCH - 1),
                                     skip_group_check=True))[-1])
        for tt in range(2):
            sl = slice(tt * 512, (tt + 1) * 512)
            def fin_v(v, tt=tt):
                v.tensor_scalar_mul(rm[:], ps[tt][0:1, :], 1.0 / D)
                return v.tensor_scalar_mul(rv[:], ps[2 + tt][0:1, :], 1.0 / D)
            p.step("vector", ["tensor"], fin_v)
            p.step("vector", [], lambda v: v.tensor_mul(rn[:], rm[:], rm[:]))
            p.step("vector", [], lambda v: v.tensor_sub(rv[:], rv[:], rn[:]))
            p.step("scalar", ["vector"], lambda s: s.activation(rv[:], rv[:], AF.Ln, bias=eps_col[0:1, :]))
            p.step("scalar", [], lambda s: s.activation(rv[:], rv[:], AF.Exp, scale=-0.5))
            p.step("vector", ["scalar"], lambda v: v.scalar_tensor_tensor(
                rn[:], rm[:], -1.0, rv[:], ALU.mult, ALU.mult))
            p.step("tensor", ["vector"],
                   lambda tn, tt=tt: (
                       tn.matmul(ps[tt][:], ones_all[0:1, :], rv[:], start=True, stop=True),
                       tn.matmul(ps[2 + tt][:], ones_all[0:1, :], rn[:], start=True, stop=True))[-1])
            p.step("scalar", ["tensor"],
                   lambda s, tt=tt, sl=sl: (
                       s.copy(rstd_b[:, sl], ps[tt][:]),
                       s.copy(nmr_b[:, sl], ps[2 + tt][:]))[-1])
        for c in range(NCH):
            p.step("sync", [("vector", p.n("vector"))],
                   lambda s, c=c: s.dma_start(
                       xc_t[:], x1T.ap().rearrange("(c p) t -> p c t", p=128)[:, c, :]),
                   inc=16)
            for tt in range(2):
                sl = slice(tt * 512, (tt + 1) * 512)
                p.step("vector", ["sync", "scalar"],
                       lambda v, sl=sl: v.tensor_mul(ps[5][:], xc_t[:, sl], rstd_b[:, sl]))
                p.step("vector", [],
                       lambda v, c=c, sl=sl: v.tensor_add(xn1[:, c, sl], ps[5][:], nmr_b[:, sl]))

        with nc.sbuf_tensor(_nm("y_sb"), [128, NCH, TC], BF16) as y_sb, \
             nc.sbuf_tensor(_nm("woT_sb"), [128, NCH, D], BF16) as woT_sb:
            if split_y:
                yf = IO["yf"]
                p.step("sync", [], lambda s: s.dma_start(m0c[:], msk0h.ap()), inc=16)
                p.step("sync", [], lambda s: s.dma_start(m1c[:], msk1h.ap()), inc=16)
                for c in range(NCH):
                    p.step("sync", [("vector", p.n("vector"))],
                           lambda s, c=c: s.dma_start(
                               ytmp0[:], yf.ap().rearrange("(c p) t -> p c t", p=128)[:, c, 0:TC]),
                           inc=16)
                    p.step("sync", [],
                           lambda s, c=c: s.dma_start(
                               ytmp1[:], yf.ap().rearrange("(c p) t -> p c t", p=128)[:, c, TC:2 * TC]),
                           inc=16)
                    p.step("vector", ["sync"],
                           lambda v: v.tensor_scalar(ytmp1[:], ytmp1[:], m1c[:], None, ALU.mult))
                    p.step("vector", [],
                           lambda v, c=c: v.scalar_tensor_tensor(
                               y_sb[:, c, :], ytmp0[:], m0c[:], ytmp1[:], ALU.mult, ALU.add))
            else:
                p.step("sync", [], lambda s: s.dma_start(
                    y_sb[:], y_full.ap().rearrange("(c p) t -> p c t", p=128)), inc=16)
            p.step("gpsimd", [], lambda g: g.dma_start(
                woT_sb[:], woT.ap().rearrange("(c p) d -> p c d", p=128)), inc=16)
            for dc in range(NCH):
                for tt in range(2):
                    par = (dc * 2 + tt) % 2
                    sl = slice(tt * 512, (tt + 1) * 512)
                    def wo_mm(tn, dc=dc, sl=sl, par=par):
                        for c in range(NCH):
                            mm = tn.matmul(ps[6 + par][:], woT_sb[:, c, dc * 128:(dc + 1) * 128],
                                           y_sb[:, c, sl], start=(c == 0), stop=(c == NCH - 1))
                        return mm
                    p.step("tensor", ["sync", "gpsimd", ("scalar", p.n("scalar"))], wo_mm)
                    p.step("scalar", ["tensor"],
                           lambda s, dc=dc, sl=sl, par=par: s.copy(
                               s1_bf[:, dc, sl], ps[6 + par][:]))

        with nc.sbuf_tensor(_nm("h1"), [128, 32, 512], BF16) as h1, \
             nc.sbuf_tensor(_nm("h2"), [128, 32, 512], BF16) as h2, \
             nc.sbuf_tensor(_nm("w1c0"), [128, NCH, 128], BF16) as w1c0, \
             nc.sbuf_tensor(_nm("w1c1"), [128, NCH, 128], BF16) as w1c1, \
             nc.sbuf_tensor(_nm("w2c0"), [128, 32, 128], BF16) as w2c0, \
             nc.sbuf_tensor(_nm("w2c1"), [128, 32, 128], BF16) as w2c1:
            w1c = [w1c0, w1c1]
            w2c = [w2c0, w2c1]

            def stream(ntask, dma_fn, mm_fn):
                after = {}
                reader = {}
                def dma(k):
                    waits = [("tensor", reader[k - 2])] if k >= 2 else \
                            [("tensor", p.n("tensor"))]
                    dma_fn(k, k % 2, waits)
                    after[k] = p.n("gpsimd")
                dma(0)
                if ntask > 1:
                    dma(1)
                for k in range(ntask):
                    mm_fn(k, k % 2, after[k])
                    reader[k] = p.n("tensor")
                    if k + 2 < ntask:
                        dma(k + 2)

            for tt in range(2):
                sl = slice(tt * 512, (tt + 1) * 512)
                def d1(k, par, waits):
                    p.step("gpsimd", waits,
                           lambda g, hc=k, par=par: g.dma_start(
                               w1c[par][:], W1T.ap()[:, hc * 128:(hc + 1) * 128]
                               .rearrange("(c p) f -> p c f", p=128)),
                           inc=16)
                def m1(k, par, aft, sl=sl):
                    def h1_mm(tn, par=par, sl=sl):
                        for c in range(NCH):
                            mm = tn.matmul(ps[par][:], w1c[par][:, c, :], xn1[:, c, sl],
                                           start=(c == 0), stop=(c == NCH - 1))
                        return mm
                    p.step("tensor", [("gpsimd", aft), ("scalar", p.n("scalar")),
                                      ("vector", p.n("vector"))], h1_mm)
                    p.step("scalar", ["tensor"],
                           lambda s, hc=k, par=par: s.activation(
                               h1[:, hc, :], ps[par][:], AF.Relu, bias=b1s[:, hc:hc + 1]))
                stream(32, d1, m1)
                def d2(k, par, waits):
                    p.step("gpsimd", waits,
                           lambda g, hc=k, par=par: g.dma_start(
                               w2c[par][:], W2T.ap()[:, hc * 128:(hc + 1) * 128]
                               .rearrange("(c p) f -> p c f", p=128)),
                           inc=16)
                def m2(k, par, aft):
                    def h2_mm(tn, par=par):
                        for q in range(32):
                            mm = tn.matmul(ps[2 + par][:], w2c[par][:, q, :], h1[:, q, :],
                                           start=(q == 0), stop=(q == 31))
                        return mm
                    p.step("tensor", [("gpsimd", aft), ("scalar", p.n("scalar"))], h2_mm)
                    p.step("scalar", ["tensor"],
                           lambda s, hc=k, par=par: s.activation(
                               h2[:, hc, :], ps[2 + par][:], AF.Relu, bias=b2s[:, hc:hc + 1]))
                stream(32, d2, m2)
                def d3(k, par, waits):
                    p.step("gpsimd", waits,
                           lambda g, dc=k, par=par: g.dma_start(
                               w2c[par][:], W3T.ap()[:, dc * 128:(dc + 1) * 128]
                               .rearrange("(c p) f -> p c f", p=128)),
                           inc=16)
                def m3(k, par, aft, sl=sl):
                    def w3_mm(tn, par=par):
                        for q in range(32):
                            mm = tn.matmul(ps[4 + par][:], w2c[par][:, q, :], h2[:, q, :],
                                           start=(q == 0), stop=(q == 31))
                        return mm
                    p.step("tensor", [("gpsimd", aft), ("scalar", p.n("scalar")),
                                      ("vector", p.n("vector")), ("sync", p.n("sync"))], w3_mm)
                    p.step("scalar", ["tensor"],
                           lambda s, dc=k, par=par: s.activation(
                               g_t[:], ps[4 + par][:], AF.Silu, bias=b3s[:, dc:dc + 1]))
                    p.step("vector", ["scalar", ("sync", p.n("sync"))],
                           lambda v, dc=k, sl=sl: v.tensor_mul(
                               pr_t[:], g_t[:], s1_bf[:, dc, sl]))
                    p.step("sync", [("vector", p.n("vector"))],
                           lambda s, dc=k, sl=sl: s.dma_start(
                               xc_t[:, 0:512],
                               x1T.ap().rearrange("(c p) t -> p c t", p=128)[:, dc, sl]),
                           inc=16)
                    p.step("vector", ["sync"],
                           lambda v: v.tensor_add(pr_t[:], pr_t[:], xc_t[:, 0:512]))
                    p.step("sync", ["vector"],
                           lambda s, dc=k, sl=sl: s.dma_start(
                               o_out.ap().rearrange("(c p) t -> p c t", p=128)[:, dc, sl],
                               pr_t[:]),
                           inc=16)
                stream(NCH, d3, m3)
        p.step("sync", ["sync"], lambda s: None, inc=0)

        if p_ext is None:
            block = e(nc.Block())
            p.emit(block)
    return nc


def _prep_D_inmaps(inputs, resB2, resC):
    woT = np.ascontiguousarray(inputs["L1_attn_wo"].T)
    W1T = np.ascontiguousarray(inputs["L1_ffn_W1"].T)
    W2T = np.ascontiguousarray(inputs["L1_ffn_W2"].T)
    W3T = np.ascontiguousarray(inputs["L1_ffn_W3"].T)
    b1r = np.ascontiguousarray(inputs["L1_ffn_b1"].reshape(32, 128).T)
    b2r = np.ascontiguousarray(inputs["L1_ffn_b2"].reshape(32, 128).T)
    b3r = np.ascontiguousarray(inputs["L1_ffn_b3"].reshape(16, 128).T)
    maps = []
    for c in range(8):
        b, h = c // 2, c % 2
        t0 = h * TC
        y_full = np.ascontiguousarray(np.concatenate(
            [resC[2 * b]["o_y"][:, t0:t0 + TC], resC[2 * b + 1]["o_y"][:, t0:t0 + TC]], 0))
        maps.append(dict(y_full=y_full, x1T=resB2[c]["o_x1"], woT=woT,
                         W1T=W1T, W2T=W2T, W3T=W3T, b1r=b1r, b2r=b2r, b3r=b3r))
    return maps


def run_D(inputs, resB2, resC):
    nc = build_D()
    res = run_bass_kernel_spmd(nc, _prep_D_inmaps(inputs, resB2, resC), core_ids=list(range(8)))
    return res.results




PAIRS = [[0, 1], [2, 3], [4, 5], [6, 7]]
_FUSED_DEBUG = False


class _Rows:

    def __init__(self, h, r0, r1):
        self.h, self.r0, self.r1 = h, r0, r1

    def ap(self):
        return self.h.ap()[self.r0:self.r1, :]


def build_fused():
    nc = bass.Bass("TRN2", target_bir_lowering=False)
    EI = lambda n, sh, dt: nc.dram_tensor(n, sh, dt, kind="ExternalInput")
    ins = {}
    for n, sh, dt in [
        ("xT", [D, TC], F32), ("xhT", [D, 3], F32), ("conv_w4", [D, 4], F32),
        ("conv_b", [128, NCH], F32), ("xprojT", [D, 160], F32),
        ("dtprojT", [DTR, D], F32), ("dtproj_b", [128, NCH], F32),
        ("A_log", [D, NST], F32), ("D_par", [128, NCH], F32),
        ("w1_all", [NEXP, D, DFF], F32), ("w2_all", [NEXP, DFF, D], F32),
        ("b1_all", [128, NEXP, NFC], F32), ("sh_b2", [128, 2, NCH], F32),
        ("rt_b2", [16, D], F32), ("cenT", [D, 16], F32), ("cen_bias", [1, 16], F32),
        ("wqT", [D, NH * HD], F32), ("wkT", [D, NKV * HD], F32), ("wvT", [D, NKV * HD], F32),
        ("woT", [D, D], F32), ("W1T", [D, MLPD], F32), ("W2T", [MLPD, MLPD], F32),
        ("W3T", [MLPD, D], F32), ("b1r", [128, 32], F32), ("b2r", [128, 32], F32),
        ("b3r", [128, 16], F32), ("msk", [128, 1], F32), ("msk0", [128, 1], F32),
        ("msk1", [128, 1], F32),
    ]:
        ins[n] = EI(n, sh, dt)
    o_out = nc.dram_tensor("o_out", [D, TC], F32, kind="ExternalOutput")
    dbg = {}
    if _FUSED_DEBUG:
        dbg = {n: nc.dram_tensor("dbg_" + n, sh, dt, kind="ExternalOutput")
               for n, sh, dt in [("xn", [D, TC], BF16), ("dl", [D, TC], F32),
                                  ("s", [D, TC], F32), ("x1", [D, TC], F32),
                                  ("y", [NH * HD, T], BF16), ("hf2", [256, NCH * NST], F32)]}
    d_xn = nc.dram_tensor("d_xn", [D, TC], BF16)
    d_xc = nc.dram_tensor("d_xc", [D, TC], BF16)
    d_dl = nc.dram_tensor("d_dl", [D, TC], F32)
    d_bc = nc.dram_tensor("d_bc", [32, TC], F32)
    d_hf = nc.dram_tensor("d_hf", [128, NCH * NST], F32)
    d_hf2 = nc.dram_tensor("d_hf2", [256, NCH * NST], F32)
    d_s = nc.dram_tensor("d_s", [D, TC], F32)
    d_x1 = nc.dram_tensor("d_x1", [D, TC], F32)
    d_x1f = nc.dram_tensor("d_x1f", [2 * D, TC], F32)
    d_y = nc.dram_tensor("d_y", [NH * HD, T], BF16)
    d_yf = nc.dram_tensor("d_yf", [2 * NH * HD, T], BF16)

    with ExitStack() as octx:
        p = Prog(nc, octx)
        IO_A = dict(xT=ins["xT"], xhT=ins["xhT"], conv_w4=ins["conv_w4"],
                    conv_b=ins["conv_b"], xprojT=ins["xprojT"], dtprojT=ins["dtprojT"],
                    dtproj_b=ins["dtproj_b"], A_log=ins["A_log"],
                    o_xn=d_xn, o_xc=d_xc, o_dl=d_dl, o_bc=d_bc, o_hf=d_hf)
        build_A(p_ext=p, IO=IO_A)
        p.step("gpsimd", ["sync"], lambda g: g.collective_compute(
            "AllGather", ALU.bypass, replica_groups=PAIRS,
            ins=[d_hf.ap().opt()], outs=[d_hf2.ap().opt()]))
        IO_B1 = dict(i_dl=d_dl, i_xc=d_xc, i_bc=d_bc, A_log=ins["A_log"],
                     D_par=ins["D_par"], i_xn=d_xn, i_hi=_Rows(d_hf2, 0, 128),
                     msk=ins["msk"], o_s=d_s)
        build_B1(p_ext=p, IO=IO_B1, masked=True)
        IO_B2 = dict(i_xn=d_xn, i_s=d_s, xT=ins["xT"], w1_all=ins["w1_all"],
                     w2_all=ins["w2_all"], b1_all=ins["b1_all"], sh_b2=ins["sh_b2"],
                     rt_b2=ins["rt_b2"], cenT=ins["cenT"], cen_bias=ins["cen_bias"],
                     o_x1=d_x1)
        build_B2(p_ext=p, IO=IO_B2)
        p.step("gpsimd", ["sync"], lambda g: g.collective_compute(
            "AllGather", ALU.bypass, replica_groups=PAIRS,
            ins=[d_x1.ap().opt()], outs=[d_x1f.ap().opt()]))
        IO_C = dict(x1a=_Rows(d_x1f, 0, D), x1b=_Rows(d_x1f, D, 2 * D),
                    wqT=ins["wqT"], wkT=ins["wkT"], wvT=ins["wvT"], o_y=d_y)
        build_C(p_ext=p, IO=IO_C)
        p.step("gpsimd", ["sync"], lambda g: g.collective_compute(
            "AllGather", ALU.bypass, replica_groups=PAIRS,
            ins=[d_y.ap().opt()], outs=[d_yf.ap().opt()]))
        IO_D = dict(yf=d_yf, x1T=d_x1, woT=ins["woT"], W1T=ins["W1T"],
                    W2T=ins["W2T"], W3T=ins["W3T"], b1r=ins["b1r"], b2r=ins["b2r"],
                    b3r=ins["b3r"], msk0=ins["msk0"], msk1=ins["msk1"], o_out=o_out)
        build_D(p_ext=p, IO=IO_D, split_y=True)
        if _FUSED_DEBUG:
            p.fence()
            for nm, src in [("xn", d_xn), ("dl", d_dl), ("s", d_s), ("x1", d_x1),
                            ("y", d_y), ("hf2", d_hf2)]:
                p.step("sync", ["sync", "gpsimd", "vector", "scalar", "tensor"],
                       lambda s, nm=nm, src=src: s.dma_start(dbg[nm].ap(), src.ap()), inc=16)
            p.step("sync", ["sync"], lambda s: None, inc=0)
        block = octx.enter_context(nc.Block())
        p.emit(block)
    return nc


def _prep_fused_inmaps(inputs):
    mA = _prep_A_inmaps(inputs)
    w1_all = np.ascontiguousarray(np.concatenate(
        [inputs["L0_ffn_shared_w1"], inputs["L0_ffn_routed_w1"]], 0))
    w2_all = np.ascontiguousarray(np.concatenate(
        [inputs["L0_ffn_shared_w2"], inputs["L0_ffn_routed_w2"]], 0))
    b1_all = np.ascontiguousarray(np.concatenate(
        [inputs["L0_ffn_shared_b1"], inputs["L0_ffn_routed_b1"]], 0)
        .reshape(NEXP, NFC, 128).transpose(2, 0, 1))
    sh_b2 = np.ascontiguousarray(
        inputs["L0_ffn_shared_b2"].reshape(2, NCH, 128).transpose(2, 0, 1))
    rt_b2 = np.ascontiguousarray(inputs["L0_ffn_routed_b2"])
    cenT = np.ascontiguousarray(inputs["L0_ffn_centroids"].T)
    cen_bias = np.ascontiguousarray(inputs["L0_ffn_bias"].reshape(1, 16))
    A_log = np.ascontiguousarray(inputs["L0_attn_A_log"])
    D_par = np.ascontiguousarray(inputs["L0_attn_D_param"].reshape(NCH, 128).T)
    woT = np.ascontiguousarray(inputs["L1_attn_wo"].T)
    W1T = np.ascontiguousarray(inputs["L1_ffn_W1"].T)
    W2T = np.ascontiguousarray(inputs["L1_ffn_W2"].T)
    W3T = np.ascontiguousarray(inputs["L1_ffn_W3"].T)
    b1r = np.ascontiguousarray(inputs["L1_ffn_b1"].reshape(32, 128).T)
    b2r = np.ascontiguousarray(inputs["L1_ffn_b2"].reshape(32, 128).T)
    b3r = np.ascontiguousarray(inputs["L1_ffn_b3"].reshape(16, 128).T)
    wq = inputs["L1_attn_wq"]
    wk = inputs["L1_attn_wk"]
    wv = inputs["L1_attn_wv"]
    maps = []
    for c in range(8):
        half = c % 2
        rows_q = np.concatenate([h * HD + _ROPE_PERM for h in range(half * NH, half * NH + NH)])
        rows_k = np.concatenate([h * HD + _ROPE_PERM for h in range(half * NKV, half * NKV + NKV)])
        rows_v = np.arange(half * NKV * HD, (half + 1) * NKV * HD)
        m = dict(mA[c])
        m.update(D_par=D_par, w1_all=w1_all, w2_all=w2_all, b1_all=b1_all,
                 sh_b2=sh_b2, rt_b2=rt_b2, cenT=cenT, cen_bias=cen_bias,
                 wqT=np.ascontiguousarray(wq[rows_q].T),
                 wkT=np.ascontiguousarray(wk[rows_k].T),
                 wvT=np.ascontiguousarray(wv[rows_v].T),
                 woT=woT, W1T=W1T, W2T=W2T, W3T=W3T, b1r=b1r, b2r=b2r, b3r=b3r,
                 msk=np.full((128, 1), float(half), np.float32),
                 msk0=np.full((128, 1), 1.0 - float(half), np.float32),
                 msk1=np.full((128, 1), float(half), np.float32))
        maps.append(m)
    return maps


def kernel_fused(inputs):
    nc = build_fused()
    res = run_bass_kernel_spmd(nc, _prep_fused_inmaps(inputs), core_ids=list(range(8)))
    out = np.empty((4, T, D), np.float32)
    for c in range(8):
        b, h = c // 2, c % 2
        out[b, h * TC:(h + 1) * TC, :] = res.results[c]["o_out"].T
    return out



def _fm(a):
    return np.ascontiguousarray(a.T)


def _prep_A_inmaps(inputs):
    x = inputs["L0_attn_x"]
    conv_w4 = np.ascontiguousarray(inputs["L0_attn_conv_w"].reshape(D, 4))
    conv_b = np.ascontiguousarray(inputs["L0_attn_conv_b"].reshape(NCH, 128).T)
    xprojT = np.ascontiguousarray(inputs["L0_attn_xproj_w"].T)
    dtprojT = np.ascontiguousarray(inputs["L0_attn_dtproj_w"].T)
    dtproj_b = np.ascontiguousarray(inputs["L0_attn_dtproj_b"].reshape(NCH, 128).T)
    A_log = np.ascontiguousarray(inputs["L0_attn_A_log"])
    maps = []
    for c in range(8):
        b, h = c // 2, c % 2
        t0 = h * TC
        xT = _fm(x[b, t0:t0 + TC])
        if h == 0:
            xhT = np.zeros((D, 3), np.float32)
        else:
            xhT = _fm(x[b, t0 - 3:t0])
        maps.append(dict(xT=xT, xhT=xhT, conv_w4=conv_w4, conv_b=conv_b,
                         xprojT=xprojT, dtprojT=dtprojT, dtproj_b=dtproj_b,
                         A_log=A_log))
    return maps


def run_A(inputs, stages=99):
    nc = build_A(stages)
    res = run_bass_kernel_spmd(nc, _prep_A_inmaps(inputs), core_ids=list(range(8)))
    return res.results


def kernel(**inputs):
    inputs = {k: np.asarray(v) for k, v in inputs.items()}
    return kernel_fused(inputs)


def kernel_unfused(**inputs):
    inputs = {k: np.asarray(v) for k, v in inputs.items()}
    resA = run_A(inputs)
    resB1 = run_B1(inputs, resA)
    resB2 = run_B2(inputs, resA, resB1)
    resC = run_C(inputs, resB2)
    resD = run_D(inputs, resB2, resC)
    out = np.empty((4, T, D), np.float32)
    for c in range(8):
        b, h = c // 2, c % 2
        out[b, h * TC:(h + 1) * TC, :] = resD[c]["o_out"].T
    return out
